# version 04
# baseline (speedup 1.0000x reference)
import contextlib
import numpy as np
import ml_dtypes
import concourse.bass as bass
import concourse.mybir as mybir
from concourse.bass_utils import run_bass_kernel_spmd

F32 = mybir.dt.float32
BF16 = mybir.dt.bfloat16
AF = mybir.ActivationFunctionType
ALU = mybir.AluOpType
AX = mybir.AxisListType

NCORES = 8
TOK = 1024
NTT = TOK // 128
D = 2048
NDT = D // 128
NPAIR = 16
FF = 1408
NHT = 11
NE = 8
KV_TOK = 8192
KV_NTT = KV_TOK // 128
EPS = 1e-5
BIG = 1.0e9
HTG = [(0, 4), (4, 8), (8, 11)]
LIMIT = 4
SUB = 99
QH = 2
QMT = 8
QLOAD = True
QDST = 0
QWAIT = True
MOEE = NE
DBG = None


def build_nc():
    nc = bass.Bass("TRN2", target_bir_lowering=False,
                   dynamic_dma_scratch_size=16384)

    def din(name, shape, dt=F32):
        return nc.dram_tensor(name, shape, dt, kind="ExternalInput").ap()

    xs = din("xs", [TOK, D])
    xkv = din("xkv", [KV_TOK, D])
    wq = din("wq", [D, D])
    wk = din("wk", [D, D])
    wv = din("wv", [D, D])
    wo = din("wo", [D, D])
    bo = din("bo", [1, D])
    gwt = din("gwt", [D, NE])
    gate_b = din("gate_b", [1, NE])
    bias_e = din("bias_e", [1, NE])
    w1t = din("w1t", [NE, D, FF])
    b1 = din("b1", [NE, FF])
    w2t = din("w2t", [NE, FF, D])
    b2 = din("b2", [NE, D])
    ident_in = din("ident", [128, 128], BF16)
    onesr_in = din("ones_row", [1, 128], BF16)
    onesc_in = din("ones_col", [128, 1], BF16)
    sel_in = din("sel", [32, D], BF16)
    sel8_in = din("sel8", [NE, NE * 128], BF16)

    out = nc.dram_tensor("out", [TOK, D], F32, kind="ExternalOutput").ap()

    AR1 = nc.alloc_sbuf_tensor("AR1", [128, 16384], F32)
    AR2 = nc.alloc_sbuf_tensor("AR2", [128, 16384], BF16)
    AR3 = nc.alloc_sbuf_tensor("AR3", [128, 16384], BF16)
    AR4 = nc.alloc_sbuf_tensor("AR4", [128, 16384], BF16)

    id_sb = nc.alloc_sbuf_tensor("id_sb", [128, 128], BF16)
    onesr = nc.alloc_sbuf_tensor("onesr", [1, 128], BF16)
    onesc = nc.alloc_sbuf_tensor("onesc", [128, 1], BF16)
    sel_sb = nc.alloc_sbuf_tensor("sel_sb", [32, D], BF16)
    sel8_sb = nc.alloc_sbuf_tensor("sel8_sb", [NE, NE * 128], BF16)
    bo_sb = nc.alloc_sbuf_tensor("bo_sb", [1, D], BF16)
    b2_sb = nc.alloc_sbuf_tensor("b2_sb", [NE, D], BF16)
    gwt_sb = nc.alloc_sbuf_tensor("gwt_sb", [128, NDT * NE], BF16)
    gbb_f = nc.alloc_sbuf_tensor("gbb_f", [1, 2 * NE], F32)
    gbb_sb = nc.alloc_sbuf_tensor("gbb_sb", [1, NE], BF16)
    b1row = nc.alloc_sbuf_tensor("b1row", [1, FF], BF16)
    b1col = nc.alloc_sbuf_tensor("b1col", [128, NHT], F32)
    kvacc = nc.alloc_sbuf_tensor("kvacc", [128, NPAIR * 128], F32)
    ksacc = nc.alloc_sbuf_tensor("ksacc", [128, NPAIR], F32)
    kvl = nc.alloc_sbuf_tensor("kvl", [128, NPAIR * 128], BF16)
    ks_sb = nc.alloc_sbuf_tensor("ks_sb", [128, NPAIR * 32], BF16)
    stats = nc.alloc_sbuf_tensor("stats", [128, 32], F32)
    rcp_f = nc.alloc_sbuf_tensor("rcp_f", [128, 32], F32)
    rcp_bf = nc.alloc_sbuf_tensor("rcp_bf", [128, NTT * 32], BF16)
    rcpT = nc.alloc_sbuf_tensor("rcpT", [32, TOK], BF16)
    crep_sb = nc.alloc_sbuf_tensor("crep_sb", [128, 1024], F32)
    comb_f = nc.alloc_sbuf_tensor("comb_f", [128, NTT * NE], F32)
    comb_bf = nc.alloc_sbuf_tensor("comb_bf", [128, NTT * NE], BF16)
    combT = nc.alloc_sbuf_tensor("combT", [NE, TOK], BF16)
    cscr = nc.alloc_sbuf_tensor("cscr", [128, 4 * NE], F32)
    knat = nc.alloc_sbuf_tensor("knat", [128, D], BF16)

    PS0 = nc.alloc_psum_tensor("PS0", [128, 2048], F32)
    PS1 = nc.alloc_psum_tensor("PS1", [128, 2048], F32)

    ctx = contextlib.ExitStack()
    sems = {}
    cnt = {}

    def getsem(name):
        if name not in sems:
            sems[name] = ctx.enter_context(nc.semaphore("sem_" + name))
            cnt[name] = 0
        return sems[name]

    for name in ("v", "a", "t", "o"):
        getsem(name)

    prog = {e: [] for e in ("sync", "gpsimd", "vector", "scalar", "tensor")}

    def emit(eng, fn):
        prog[eng].append(fn)

    def plan(sem, n=1):
        getsem(sem)
        cnt[sem] += n
        return cnt[sem]

    def wait(eng, sem, val):
        if val <= 0:
            return
        getsem(sem)
        assert val <= cnt[sem], (eng, sem, val, cnt[sem])
        if sem not in ("v", "a", "t"):
            assert val == cnt[sem] or val <= 0, (eng, sem, val, cnt[sem])
        if val <= 0:
            return
        h = sems[sem]
        emit(eng, lambda E, h=h, v=val: E.wait_ge(h, v))

    def run(eng, sem, fn, n=1):
        getsem(sem)
        v = plan(sem, n)
        h = sems[sem]
        emit(eng, lambda E, h=h, n=n, fn=fn: fn(E).then_inc(h, n))
        return v

    OWNER = {"t": "tensor", "v": "vector", "a": "scalar"}

    def gate_gpsimd(*conds):
        getsem("gg")
        for sem, val in conds:
            if val <= 0:
                continue
            assert sem in OWNER
            assert val <= cnt[sem]
            eng = OWNER[sem]
            hown = sems[sem]
            emit(eng, lambda E, hown=hown, v=val: E.wait_ge(hown, v))
            hgg = sems["gg"]
            plan("gg", 1)
            emit(eng, lambda E, hgg=hgg: E.nop().then_inc(hgg, 1))
        wait("gpsimd", "gg", cnt["gg"])

    wk_res = AR1[:].bitcast(BF16)

    def wv_chunk(dt):
        if dt < 8:
            return AR2[:, 2048 * dt:2048 * dt + 2048]
        return AR3[:, 2048 * (dt - 8):2048 * (dt - 8) + 2048]

    def x_tile(tt):
        return AR1[:, 2048 * tt:2048 * tt + 2048]

    def lnT_dt(dt):
        return AR2[:, 1024 * dt:1024 * dt + 1024]

    def qT_pair(p):
        return AR3[:, 1024 * p:1024 * p + 1024]

    def hs_ht(ht):
        return AR3[:, 1024 * ht:1024 * ht + 1024]

    ar4f = AR4[:].bitcast(F32)
    vnat = crep_sb[:].bitcast(BF16)

    def xk_slot(r):
        return ar4f[:, 2048 * r:2048 * r + 2048]

    def lnN_kv_slot(r):
        return AR4[:, 8192 + 2048 * r:8192 + 2048 * r + 2048]

    def lnTt_slot(r):
        return AR4[:, 12288 + 2048 * r:12288 + 2048 * r + 2048]

    def wq_half_dt(dt):
        if QDST == 1:
            return AR3[:, 1024 * (dt % 4) + 4096:1024 * (dt % 4) + 5120]
        return AR4[:, 1024 * dt:1024 * dt + 1024]

    def wo_slot(r):
        return AR4[:, 2048 * r:2048 * r + 2048]

    def w1_slot(r):
        return AR4[:, 1408 * r:1408 * r + 1408]

    def w2_slot(r):
        return AR4[:, 2816 + 2048 * r:2816 + 2048 * r + 2048]

    def silu_slot(ssl):
        return ar4f[:, 5504 + 1024 * ssl:5504 + 1024 * ssl + 1024]

    def lnN_own_slot(r):
        return knat[:] if r == 0 else vnat

    def bankv(ps, b, n=1):
        return ps[:, 512 * b:512 * b + 512 * n]

    run("sync", "sc", lambda E: E.dma_start(out=id_sb[:], in_=ident_in), 16)
    run("sync", "sc", lambda E: E.dma_start(out=onesr[:], in_=onesr_in), 16)
    run("sync", "sc", lambda E: E.dma_start(out=onesc[:], in_=onesc_in), 16)
    run("sync", "sc", lambda E: E.dma_start(out=sel_sb[:], in_=sel_in), 16)
    run("sync", "sc", lambda E: E.dma_start(out=sel8_sb[:], in_=sel8_in), 16)
    run("sync", "sc", lambda E: E.dma_start(out=gbb_f[:, 0:NE], in_=gate_b), 16)
    s_consts = run("sync", "sc",
                   lambda E: E.dma_start(out=gbb_f[:, NE:2 * NE], in_=bias_e), 16)

    run("gpsimd", "ss", lambda E: E.dma_start(out=bo_sb[:], in_=bo), 16)
    run("gpsimd", "ss", lambda E: E.dma_start(out=b2_sb[:], in_=b2), 16)
    g_small = run("gpsimd", "ss", lambda E: E.dma_start(
        out=gwt_sb[:].rearrange("p (dt e) -> p dt e", dt=NDT),
        in_=gwt.rearrange("(dt p) e -> p dt e", p=128)), 16)

    wait("vector", "sc", s_consts)
    run("vector", "v", lambda E: E.tensor_add(
        gbb_f[:, 0:NE], gbb_f[:, 0:NE], gbb_f[:, NE:2 * NE]))
    wait("vector", "v", cnt["v"])
    v_gbb = run("vector", "v", lambda E: E.tensor_copy(gbb_sb[:], gbb_f[:, 0:NE]))

    run("vector", "v", lambda E: E.memset(kvl[:], 0.0))
    v_zero = run("vector", "v", lambda E: E.memset(ks_sb[:], 0.0))

    for dt in range(NDT):
        run("gpsimd", "swkv", lambda E, dt=dt: E.dma_start(
            out=wk_res[:, 2048 * dt:2048 * dt + 2048],
            in_=wk[128 * dt:128 * dt + 128, :]), 16)
    for dt in range(NDT):
        run("gpsimd", "swkv", lambda E, dt=dt: E.dma_start(
            out=wv_chunk(dt), in_=wv[128 * dt:128 * dt + 128, :]), 16)
    g_wkv = cnt["swkv"]

    ln_prev = {"v": 0, "a": 0}

    def emit_ln_tile(src_ap, nat_slot, dst_writer):
        for c in range(4):
            run("vector", "v", lambda E, c=c: E.bn_stats(
                stats[:, 6 * c:6 * c + 6], src_ap[:, 512 * c:512 * c + 512]))
        wait("vector", "v", cnt["v"])
        run("vector", "v", lambda E: E.bn_aggr(stats[:, 24:26], stats[:, 0:24]))
        v_mv = cnt["v"]
        wait("vector", "v", v_mv)
        run("vector", "v", lambda E: E.tensor_scalar_add(
            stats[:, 30:31], stats[:, 25:26], EPS))
        v_ve = cnt["v"]
        wait("scalar", "v", v_ve)
        run("scalar", "a", lambda E: E.activation(
            stats[:, 26:27], stats[:, 30:31], AF.Sqrt))
        wait("vector", "a", cnt["a"])
        run("vector", "v", lambda E: E.reciprocal(stats[:, 27:28], stats[:, 26:27]))
        wait("vector", "v", cnt["v"])
        run("vector", "v", lambda E: E.tensor_mul(
            stats[:, 28:29], stats[:, 24:25], stats[:, 27:28]))
        wait("vector", "v", cnt["v"])
        v_ab = run("vector", "v", lambda E: E.tensor_scalar_mul(
            stats[:, 29:30], stats[:, 28:29], -1.0))
        wait("scalar", "v", v_ab)
        a_ln = run("scalar", "a", lambda E: E.activation(
            nat_slot, src_ap, AF.Identity,
            bias=stats[:, 29:30], scale=stats[:, 27:28]))
        wait("tensor", "a", a_ln)
        for j in range(4):
            ps = bankv(PS0, 3) if j % 2 == 0 else bankv(PS1, 3)
            eng = "vector" if j % 2 == 0 else "scalar"
            ek = "v" if j % 2 == 0 else "a"
            wait("tensor", ek, ln_prev[ek])
            for u in range(4):
                dtj = 4 * j + u
                run("tensor", "t", lambda E, ps=ps, u=u, dtj=dtj: E.matmul(
                    ps[:, 128 * u:128 * u + 128],
                    nat_slot[:, 128 * dtj:128 * dtj + 128],
                    id_sb[:], start=True, stop=True))
            t_pack = cnt["t"]
            wait(eng, "t", t_pack)
            dst_writer(j, ps, eng)
            ln_prev[ek] = cnt[ek]
        return a_ln

    xk_lnread_a = {}
    lnTt_done_t = {}
    kv_add_done = {}

    kv_range = range(KV_NTT if LIMIT >= 1 else 0)
    for kt in kv_range:
        r = kt % 2
        if kt >= 2:
            wait("sync", "a", xk_lnread_a[kt - 2])
        sxn = "sx%d" % r
        s_x = run("sync", sxn, lambda E, kt=kt, r=r: E.dma_start(
            out=xk_slot(r), in_=xkv[128 * kt:128 * kt + 128, :]), 16)
        wait("vector", sxn, s_x)
        wait("scalar", sxn, s_x)
        if kt >= 2:
            tprev = lnTt_done_t[kt - 2]
            wait("vector", "t", tprev)
            wait("scalar", "t", tprev)
        lt = kt % 2

        def lnTt_writer(j, ps, eng, lt=lt):
            dst = lnTt_slot(lt)[:, 512 * j:512 * j + 512]
            if eng == "vector":
                run("vector", "v", lambda E, ps=ps, dst=dst: E.tensor_copy(dst, ps))
            else:
                run("scalar", "a", lambda E, ps=ps, dst=dst: E.copy(dst, ps))

        a_ln = emit_ln_tile(xk_slot(r), lnN_kv_slot(lt), lnTt_writer)
        xk_lnread_a[kt] = a_ln
        v_lnTt = cnt["v"]
        a_lnTt = cnt["a"]

        if kt == 0:
            wait("tensor", "sc", s_consts)
            wait("tensor", "swkv", g_wkv)
        if kt >= 1:
            wait("tensor", "v", kv_add_done[kt - 1])
        wait("tensor", "v", v_lnTt)
        wait("tensor", "a", a_lnTt)
        for dt in range(NDT):
            lhs = lnTt_slot(lt)[:, 128 * dt:128 * dt + 128]
            for c in range(4):
                run("tensor", "t", lambda E, lhs=lhs, dt=dt, c=c: E.matmul(
                    bankv(PS0, c), lhs,
                    wk_res[:, 2048 * dt + 512 * c:2048 * dt + 512 * c + 512],
                    start=(dt == 0), stop=(dt == NDT - 1)))
            for c in range(4):
                run("tensor", "t", lambda E, lhs=lhs, dt=dt, c=c: E.matmul(
                    bankv(PS1, c), lhs,
                    wv_chunk(dt)[:, 512 * c:512 * c + 512],
                    start=(dt == 0), stop=(dt == NDT - 1)))
        t_proj = cnt["t"]
        lnTt_done_t[kt] = t_proj

        wait("scalar", "t", t_proj)
        a_k = run("scalar", "a", lambda E: E.activation(knat[:], PS0[:], AF.Relu))
        wait("vector", "t", t_proj)
        v_v = run("vector", "v", lambda E: E.tensor_copy(vnat, PS1[:]))

        wait("tensor", "a", a_k)
        wait("tensor", "v", v_v)
        for p in range(NPAIR):
            ps = PS0 if p < 8 else PS1
            q = p % 8
            run("tensor", "t", lambda E, ps=ps, p=p, q=q: E.matmul(
                ps[:, 128 * q:128 * q + 128],
                knat[:, 128 * p:128 * p + 128],
                vnat[:, 128 * p:128 * p + 128], start=True, stop=True))
            run("tensor", "t", lambda E, ps=ps, p=p, q=q: E.matmul(
                ps[:, 1024 + q:1024 + q + 1],
                knat[:, 128 * p:128 * p + 128],
                onesc[:], start=True, stop=True))
        t_kv = cnt["t"]

        wait("vector", "t", t_kv)
        if kt == 0:
            run("vector", "v", lambda E: E.tensor_copy(kvacc[:, 0:1024],
                                                       PS0[:, 0:1024]))
            run("vector", "v", lambda E: E.tensor_copy(kvacc[:, 1024:2048],
                                                       PS1[:, 0:1024]))
            run("vector", "v", lambda E: E.tensor_copy(ksacc[:, 0:8],
                                                       PS0[:, 1024:1032]))
            run("vector", "v", lambda E: E.tensor_copy(ksacc[:, 8:16],
                                                       PS1[:, 1024:1032]))
        else:
            run("vector", "v", lambda E: E.tensor_add(
                kvacc[:, 0:1024], PS0[:, 0:1024], kvacc[:, 0:1024]))
            run("vector", "v", lambda E: E.tensor_add(
                kvacc[:, 1024:2048], PS1[:, 0:1024], kvacc[:, 1024:2048]))
            run("vector", "v", lambda E: E.tensor_add(
                ksacc[:, 0:8], PS0[:, 1024:1032], ksacc[:, 0:8]))
            run("vector", "v", lambda E: E.tensor_add(
                ksacc[:, 8:16], PS1[:, 1024:1032], ksacc[:, 8:16]))
        kv_add_done[kt] = cnt["v"]

    t_kv_final = cnt["t"]
    v_kv_final = cnt["v"]

    wait("vector", "v", v_kv_final)
    wait("vector", "v", v_zero)
    for p in range(NPAIR if LIMIT >= 1 else 0):
        c0 = 128 * p
        run("vector", "v", lambda E, c0=c0: E.tensor_copy(
            kvl[0:64, c0:c0 + 64], kvacc[0:64, c0:c0 + 64]))
        run("vector", "v", lambda E, c0=c0: E.tensor_copy(
            kvl[64:128, c0 + 64:c0 + 128], kvacc[64:128, c0 + 64:c0 + 128]))
        k0 = 32 * p
        run("vector", "v", lambda E, p=p, k0=k0: E.tensor_copy(
            ks_sb[0:64, k0 + 2 * p:k0 + 2 * p + 1], ksacc[0:64, p:p + 1]))
        run("vector", "v", lambda E, p=p, k0=k0: E.tensor_copy(
            ks_sb[64:128, k0 + 2 * p + 1:k0 + 2 * p + 2],
            ksacc[64:128, p:p + 1]))
    v_kvl = cnt["v"]

    wait("sync", "t", t_kv_final)
    s_xs = run("sync", "sxs", lambda E: E.dma_start(
        out=AR1[:].rearrange("p (tt d) -> p tt d", tt=NTT),
        in_=xs.rearrange("(tt p) d -> p tt d", p=128)), 16)

    wait("vector", "t", t_kv_final)
    wait("scalar", "t", t_kv_final)
    wait("vector", "sxs", s_xs)
    wait("scalar", "sxs", s_xs)
    x_res_done = {tt: 0 for tt in range(NTT)}
    ln_marks = {}

    def emit_own_ln(tt, prev_mark):
        lt = tt % 2

        def wtr(j, ps, eng, tt=tt):
            for u in range(4):
                dtj = 4 * j + u
                dst = lnT_dt(dtj)[:, 128 * tt:128 * tt + 128]
                src = ps[:, 128 * u:128 * u + 128]
                if eng == "vector":
                    run("vector", "v", lambda E, s=src, d=dst: E.tensor_copy(d, s))
                else:
                    run("scalar", "a", lambda E, s=src, d=dst: E.copy(d, s))

        if prev_mark is not None:
            wait("vector", "t", prev_mark)
            wait("scalar", "t", prev_mark)
        emit_ln_tile(x_tile(tt), lnN_own_slot(lt), wtr)
        return cnt["t"]

    for tt in range(NTT if LIMIT >= 2 else 0):
        ln_marks[tt] = emit_own_ln(tt, ln_marks.get(tt - 2))
    v_ln1 = cnt["v"]
    a_ln1 = cnt["a"]
    if LIMIT < 2:
        emit_attn = False
    else:
        emit_attn = True

    class _StopAttn(Exception):
        pass

    v_x_attn = 0
    try:
        if not (emit_attn and SUB >= 1):
                raise _StopAttn()
        gate_gpsimd(("a", xk_lnread_a.get(KV_NTT - 1, 0)),
                    ("t", t_kv_final))
        q_evac_a = {}
        for h in range(QH):
            if h == 1:
                gate_gpsimd(("t", cnt["t"]))
            if QLOAD:
                for dt in range(NDT):
                    run("gpsimd", "swq", lambda E, h=h, dt=dt: E.dma_start(
                        out=wq_half_dt(dt),
                        in_=wq[128 * dt:128 * dt + 128, 1024 * h:1024 * h + 1024]), 16)
            g_wq = cnt.get("swq", 0)
            if g_wq:
                wait("tensor", "swq", g_wq)
            if h == 0:
                wait("tensor", "v", v_ln1)
                wait("tensor", "a", a_ln1)
            for m in range(QMT):
                pg = 8 * h + m
                ps = PS0 if pg % 2 == 0 else PS1
                if pg >= 2:
                    wait("tensor", "a", q_evac_a[pg - 2])
                for dt in range(NDT):
                    lhs = wq_half_dt(dt)[:, 128 * m:128 * m + 128]
                    for c in range(2):
                        run("tensor", "t", lambda E, ps=ps, lhs=lhs, dt=dt, c=c:
                            E.matmul(bankv(ps, c), lhs,
                                     lnT_dt(dt)[:, 512 * c:512 * c + 512],
                                     start=(dt == 0), stop=(dt == NDT - 1)))
                t_q = cnt["t"]
                wait("scalar", "t", t_q)
                q_evac_a[pg] = run("scalar", "a", lambda E, ps=ps, pg=pg:
                                   E.activation(qT_pair(pg), ps[:, 0:1024], AF.Relu))
        a_q = cnt["a"]

        if SUB < 2:
            raise _StopAttn()
        wait("tensor", "a", a_q)
        wait("tensor", "v", v_kvl)
        for tt in range(NTT):
            for p in range(NPAIR):
                run("tensor", "t", lambda E, tt=tt, p=p: E.matmul(
                    PS1[:, 1536 + 32 * tt:1536 + 32 * tt + 32],
                    qT_pair(p)[:, 128 * tt:128 * tt + 128],
                    ks_sb[:, 32 * p:32 * p + 32],
                    start=(p == 0), stop=(p == NPAIR - 1)))
        t_norm = cnt["t"]

        wait("vector", "t", t_norm)
        for tt in range(NTT):
            wait("vector", "v", cnt["v"])
            run("vector", "v", lambda E, tt=tt: E.tensor_scalar_max(
                rcp_f[:], PS1[:, 1536 + 32 * tt:1536 + 32 * tt + 32], 1e-6))
            wait("vector", "v", cnt["v"])
            run("vector", "v", lambda E: E.reciprocal(rcp_f[:], rcp_f[:]))
            wait("vector", "v", cnt["v"])
            run("vector", "v", lambda E, tt=tt: E.tensor_copy(
                rcp_bf[:, 32 * tt:32 * tt + 32], rcp_f[:]))
        v_rcp = cnt["v"]

        wait("tensor", "v", v_rcp)
        for tt in range(NTT):
            run("tensor", "t", lambda E, tt=tt: E.matmul(
                PS1[0:32, 1536:1664], rcp_bf[:, 32 * tt:32 * tt + 32],
                id_sb[:], start=True, stop=True))
            t_r = cnt["t"]
            wait("vector", "t", t_r)
            v_r = run("vector", "v", lambda E, tt=tt: E.tensor_copy(
                rcpT[0:32, 128 * tt:128 * tt + 128], PS1[0:32, 1536:1664]))
            wait("tensor", "v", v_r)
        v_rcpT = cnt["v"]

        if SUB < 3:
            raise _StopAttn()
        attnT_v = {}
        rep_a = {}
        for p in range(NPAIR):
            ps_q = PS0 if p % 2 == 0 else PS1
            if p >= 2:
                wait("tensor", "v", attnT_v[p - 2])
            for c in range(2):
                run("tensor", "t", lambda E, ps_q=ps_q, p=p, c=c: E.matmul(
                    bankv(ps_q, c), kvl[:, 128 * p:128 * p + 128],
                    qT_pair(p)[:, 512 * c:512 * c + 512], start=True, stop=True))
            for c in range(2):
                run("tensor", "t", lambda E, ps_q=ps_q, p=p, c=c: E.matmul(
                    ps_q[:, 1024 + 512 * c:1024 + 512 * c + 512],
                    sel_sb[:, 128 * p:128 * p + 128],
                    rcpT[0:32, 512 * c:512 * c + 512], start=True, stop=True))
            t_p = cnt["t"]
            wait("scalar", "t", t_p)
            if p >= 1:
                wait("scalar", "v", attnT_v[p - 1])
            rep_a[p] = run("scalar", "a", lambda E, ps_q=ps_q: E.copy(
                ar4f[:, 0:1024], ps_q[:, 1024:2048]))
            wait("vector", "t", t_p)
            wait("vector", "a", rep_a[p])
            attnT_v[p] = run("vector", "v", lambda E, ps_q=ps_q, p=p: E.tensor_mul(
                qT_pair(p), ps_q[:, 0:1024], ar4f[:, 0:1024]))
        v_attnT = cnt["v"]

        if SUB < 4:
            raise _StopAttn()
        gate_gpsimd(("t", cnt["t"]), ("v", v_attnT))
        x_res_done = {}
        wo_cons = {}
        wait("tensor", "v", v_attnT)
        for g in range(4):
            if g >= 1:
                wait("tensor", "v", x_res_done[2 * g - 1])
            wait("tensor", "ss", g_small)
            for u in range(2):
                ps = PS0 if u == 0 else PS1
                for c in range(4):
                    run("tensor", "t", lambda E, ps=ps, c=c: E.matmul(
                        bankv(ps, c), onesr[0:1, :],
                        bo_sb[0:1, 512 * c:512 * c + 512], start=True, stop=False))
            for p in range(NPAIR):
                idx = g * NPAIR + p
                pm = wo_cons.get(idx - 4)
                if pm is not None:
                    gate_gpsimd(("t", pm))
                swon = "swo%d" % (p % 4)
                gm = run("gpsimd", swon, lambda E, p=p: E.dma_start(
                    out=wo_slot(p % 4), in_=wo[128 * p:128 * p + 128, :]), 16)
                wait("tensor", swon, gm)
                for u in range(2):
                    tt = 2 * g + u
                    ps = PS0 if u == 0 else PS1
                    lhs = qT_pair(p)[:, 128 * tt:128 * tt + 128]
                    last = (p == NPAIR - 1)
                    for c in range(4):
                        run("tensor", "t",
                            lambda E, ps=ps, lhs=lhs, p=p, c=c, last=last:
                            E.matmul(bankv(ps, c), lhs,
                                     wo_slot(p % 4)[:, 512 * c:512 * c + 512],
                                     start=False, stop=last))
                wo_cons[idx] = cnt["t"]
            t_g = cnt["t"]
            for u in range(2):
                tt = 2 * g + u
                ps = PS0 if u == 0 else PS1
                wait("vector", "t", t_g)
                wait("vector", "sxs", s_xs)
                x_res_done[tt] = run("vector", "v", lambda E, ps=ps, tt=tt:
                                     E.tensor_add(x_tile(tt), ps[:], x_tile(tt)))
        v_x_attn = cnt["v"]
    except _StopAttn:
        pass

    wait("tensor", "v", v_x_attn)
    ln2_marks = {}
    for tt in range(NTT if LIMIT >= 3 else 0):
        lt = tt % 2

        def wtr2(j, ps, eng, tt=tt):
            for u in range(4):
                dtj = 4 * j + u
                dst = lnT_dt(dtj)[:, 128 * tt:128 * tt + 128]
                src = ps[:, 128 * u:128 * u + 128]
                if eng == "vector":
                    run("vector", "v", lambda E, s=src, d=dst: E.tensor_copy(d, s))
                else:
                    run("scalar", "a", lambda E, s=src, d=dst: E.copy(d, s))

        wait("vector", "v", x_res_done[tt])
        wait("scalar", "v", x_res_done[tt])
        pm = ln2_marks.get(tt - 2)
        if pm is not None:
            wait("vector", "t", pm)
            wait("scalar", "t", pm)
        emit_ln_tile(x_tile(tt), lnN_own_slot(lt), wtr2)
        ln2_marks[tt] = cnt["t"]
    v_ln2 = cnt["v"]
    a_ln2 = cnt["a"]

    wait("tensor", "v", v_ln2)
    wait("tensor", "a", a_ln2)
    wait("tensor", "v", v_gbb)
    comb_read_v = {}
    for tt in range(NTT if LIMIT >= 3 else 0):
        if tt >= 1:
            wait("tensor", "v", comb_read_v[tt - 1])
        for dt in range(NDT):
            run("tensor", "t", lambda E, tt=tt, dt=dt: E.matmul(
                PS0[:, 8 * tt:8 * tt + 8],
                lnT_dt(dt)[:, 128 * tt:128 * tt + 128],
                gwt_sb[:, 8 * dt:8 * dt + 8],
                start=(dt == 0), stop=False))
        run("tensor", "t", lambda E, tt=tt: E.matmul(
            PS0[:, 8 * tt:8 * tt + 8], onesr[0:1, :],
            gbb_sb[0:1, :], start=False, stop=True))
        t_sc = cnt["t"]
        sc = PS0[:, 8 * tt:8 * tt + 8]
        m1 = cscr[:, 0:1]
        m2 = cscr[:, 1:2]
        g1 = cscr[:, 2:3]
        g2 = cscr[:, 3:4]
        is1 = cscr[:, 8:16]
        msk = cscr[:, 16:24]
        t2 = cscr[:, 24:32]
        wait("vector", "t", t_sc)
        run("vector", "v", lambda E, sc=sc, m1=m1: E.reduce_max(m1, sc, axis=AX.X))
        wait("vector", "v", cnt["v"])
        run("vector", "v", lambda E, sc=sc, is1=is1, m1=m1: E.tensor_scalar(
            is1, sc, m1, None, ALU.is_ge))
        wait("vector", "v", cnt["v"])
        comb_read_v[tt] = run("vector", "v",
                              lambda E, sc=sc, is1=is1, msk=msk:
                              E.scalar_tensor_tensor(msk, is1, -BIG, sc,
                                                     ALU.mult, ALU.add))
        wait("vector", "v", cnt["v"])
        run("vector", "v", lambda E, msk=msk, m2=m2: E.reduce_max(
            m2, msk, axis=AX.X))
        wait("vector", "v", cnt["v"])
        v_d = run("vector", "v", lambda E, m1=m1, m2=m2, g1=g1:
                  E.tensor_sub(g1, m1, m2))
        wait("scalar", "v", v_d)
        a_g1 = run("scalar", "a", lambda E, g1=g1: E.activation(
            g1, g1, AF.Sigmoid))
        wait("vector", "a", a_g1)
        run("vector", "v", lambda E, g1=g1, g2=g2: E.tensor_scalar(
            g2, g1, -1.0, 1.0, ALU.mult, ALU.add))
        wait("vector", "v", cnt["v"])
        run("vector", "v", lambda E, is1=is1, g1=g1: E.tensor_scalar(
            is1, is1, g1, None, ALU.mult))
        run("vector", "v", lambda E, msk=msk, m2=m2, t2=t2: E.tensor_scalar(
            t2, msk, m2, None, ALU.is_ge))
        wait("vector", "v", cnt["v"])
        run("vector", "v", lambda E, t2=t2, g2=g2: E.tensor_scalar(
            t2, t2, g2, None, ALU.mult))
        wait("vector", "v", cnt["v"])
        run("vector", "v", lambda E, tt=tt, is1=is1, t2=t2: E.tensor_add(
            comb_f[:, 8 * tt:8 * tt + 8], is1, t2))
        wait("vector", "v", cnt["v"])
        run("vector", "v", lambda E, tt=tt: E.tensor_copy(
            comb_bf[:, 8 * tt:8 * tt + 8], comb_f[:, 8 * tt:8 * tt + 8]))
    v_comb = cnt["v"]

    wait("tensor", "v", v_comb)
    for tt in range(NTT if LIMIT >= 3 else 0):
        run("tensor", "t", lambda E, tt=tt: E.matmul(
            PS1[0:8, 0:128], comb_bf[:, 8 * tt:8 * tt + 8], id_sb[:],
            start=True, stop=True))
        t_c = cnt["t"]
        wait("vector", "t", t_c)
        v_c = run("vector", "v", lambda E, tt=tt: E.tensor_copy(
            combT[0:8, 128 * tt:128 * tt + 128], PS1[0:8, 0:128]))
        wait("tensor", "v", v_c)
    v_combT = cnt["v"]

    gate_gpsimd(("t", cnt["t"]))
    wait("vector", "a", a_ln2)

    w1_ring_mark = {}
    w2_ring_mark = {}
    x_moe_done = dict(x_res_done)
    prev_hs_v = None
    b1_read_t = None
    silu_mark = {}
    for e in range(MOEE if LIMIT >= 4 else 0):
        wait("tensor", "v", cnt["v"])
        if b1_read_t is not None:
            gate_gpsimd(("t", b1_read_t))
        g_b1 = run("gpsimd", "sb1", lambda E, e=e: E.dma_start(
            out=b1row[:], in_=b1[e:e + 1, :]), 16)

        run("tensor", "t", lambda E, e=e: E.matmul(
            PS0[:, 0:512], sel8_sb[:, 128 * e:128 * e + 128],
            combT[0:8, 0:512], start=True, stop=True))
        run("tensor", "t", lambda E, e=e: E.matmul(
            PS0[:, 512:1024], sel8_sb[:, 128 * e:128 * e + 128],
            combT[0:8, 512:1024], start=True, stop=True))
        t_rep = cnt["t"]
        wait("scalar", "t", t_rep)
        if prev_hs_v is not None:
            wait("scalar", "v", prev_hs_v)
        a_crep = run("scalar", "a", lambda E: E.copy(crep_sb[:], PS0[:, 0:1024]))

        wait("tensor", "sb1", g_b1)
        for ht in range(NHT):
            run("tensor", "t", lambda E, ht=ht: E.matmul(
                PS1[:, ht:ht + 1], b1row[0:1, 128 * ht:128 * ht + 128],
                onesc[0:1, 0:1], start=True, stop=True))
        b1_read_t = cnt["t"]
        wait("vector", "t", b1_read_t)
        v_b1 = run("vector", "v", lambda E: E.tensor_copy(b1col[:], PS1[:, 0:NHT]))

        wait("tensor", "a", a_crep)
        wait("tensor", "v", v_b1)
        hs_v = {}
        for gi, (h0, h1) in enumerate(HTG):
            nsl = h1 - h0
            slots = [(PS0, 0), (PS0, 2), (PS1, 0), (PS1, 2)][:nsl]
            for dt in range(NDT):
                i = (e * len(HTG) + gi) * NDT + dt
                slot = i % 2
                pm = w1_ring_mark.get(i - 2)
                if pm is not None:
                    gate_gpsimd(("t", pm))
                sw1n = "sw1_%d" % slot
                gm = run("gpsimd", sw1n, lambda E, e=e, dt=dt, slot=slot:
                         E.dma_start(out=w1_slot(slot),
                                     in_=w1t[e, 128 * dt:128 * dt + 128, :]), 16)
                wait("tensor", sw1n, gm)
                for si, ht in enumerate(range(h0, h1)):
                    ps, b0 = slots[si]
                    lhs = w1_slot(slot)[:, 128 * ht:128 * ht + 128]
                    for c in range(2):
                        run("tensor", "t",
                            lambda E, ps=ps, b0=b0, lhs=lhs, dt=dt, c=c:
                            E.matmul(
                                ps[:, 512 * (b0 + c):512 * (b0 + c) + 512],
                                lhs, lnT_dt(dt)[:, 512 * c:512 * c + 512],
                                start=(dt == 0), stop=(dt == NDT - 1)))
                w1_ring_mark[i] = cnt["t"]
            t_w1g = cnt["t"]
            for si, ht in enumerate(range(h0, h1)):
                ps, b0 = slots[si]
                ssl = si % 2
                st = silu_slot(ssl)
                wait("scalar", "t", t_w1g)
                wait("scalar", "v", v_b1)
                sm = silu_mark.get(ssl)
                if sm is not None:
                    wait("scalar", "v", sm)
                a_s = run("scalar", "a", lambda E, ps=ps, b0=b0, st=st, ht=ht:
                          E.activation(st, ps[:, 512 * b0:512 * b0 + 1024],
                                       AF.Silu, bias=b1col[:, ht:ht + 1]))
                wait("vector", "a", a_s)
                hs_v[ht] = run("vector", "v", lambda E, st=st, ht=ht:
                               E.tensor_mul(hs_ht(ht), st, crep_sb[:]))
                silu_mark[ssl] = hs_v[ht]
            wait("tensor", "a", cnt["a"])
        prev_hs_v = cnt["v"]

        wait("tensor", "v", prev_hs_v)
        for ttg in range(4):
            if e > 0 or ttg > 0:
                wait("tensor", "v", cnt["v"])
            if e == 0:
                for u in range(2):
                    tt = 2 * ttg + u
                    ps = PS0 if u == 0 else PS1
                    for c in range(4):
                        run("tensor", "t", lambda E, ps=ps, tt=tt, c=c:
                            E.matmul(bankv(ps, c),
                                     combT[0:8, 128 * tt:128 * tt + 128],
                                     b2_sb[0:8, 512 * c:512 * c + 512],
                                     start=True, stop=False))
            for ht in range(NHT):
                i = (e * 4 + ttg) * NHT + ht
                slot = i % 4
                pm = w2_ring_mark.get(i - 4)
                if pm is not None:
                    gate_gpsimd(("t", pm))
                sw2n = "sw2_%d" % slot
                gm = run("gpsimd", sw2n, lambda E, e=e, ht=ht, slot=slot:
                         E.dma_start(out=w2_slot(slot),
                                     in_=w2t[e, 128 * ht:128 * ht + 128, :]), 16)
                wait("tensor", sw2n, gm)
                for u in range(2):
                    tt = 2 * ttg + u
                    ps = PS0 if u == 0 else PS1
                    lhs = hs_ht(ht)[:, 128 * tt:128 * tt + 128]
                    first = (e > 0 and ht == 0)
                    last = (ht == NHT - 1)
                    for c in range(4):
                        run("tensor", "t",
                            lambda E, ps=ps, lhs=lhs, slot=slot, c=c,
                            first=first, last=last:
                            E.matmul(bankv(ps, c), lhs,
                                     w2_slot(slot)[:, 512 * c:512 * c + 512],
                                     start=first, stop=last))
                w2_ring_mark[i] = cnt["t"]
            t_w2g = cnt["t"]
            for u in range(2):
                tt = 2 * ttg + u
                ps = PS0 if u == 0 else PS1
                wait("vector", "t", t_w2g)
                wait("vector", "v", x_moe_done[tt])
                x_moe_done[tt] = run("vector", "v", lambda E, ps=ps, tt=tt:
                                     E.tensor_add(x_tile(tt), ps[:],
                                                  x_tile(tt)))

    if DBG is not None:
        dbg_map = {"AR2": (AR2, BF16, 16384), "AR3": (AR3, BF16, 16384),
                   "AR1": (AR1, F32, 16384),
                   "kvacc": (kvacc, F32, NPAIR * 128),
                   "ksacc": (ksacc, F32, NPAIR),
                   "rcpT": (rcpT, BF16, TOK)}
        dt_, ddt, dcols = dbg_map[DBG]
        dbg_out = nc.dram_tensor("dbg", list(dt_.shape), ddt,
                                 kind="ExternalOutput").ap()
        wait("sync", "v", cnt["v"])
        wait("sync", "a", cnt["a"])
        wait("sync", "t", cnt["t"])
        run("sync", "o", lambda E: E.dma_start(out=dbg_out, in_=dt_[:]), 16)
    for tt in range(NTT):
        wait("sync", "v", x_moe_done[tt])
        run("sync", "o", lambda E, tt=tt: E.dma_start(
            out=out[128 * tt:128 * tt + 128, :], in_=x_tile(tt)), 16)
    wait("sync", "o", cnt["o"])

    with nc.Block() as block:
        @block.sync
        def _(E):
            for fn in prog["sync"]:
                fn(E)

        @block.gpsimd
        def _(E):
            for fn in prog["gpsimd"]:
                fn(E)

        @block.vector
        def _(E):
            for fn in prog["vector"]:
                fn(E)

        @block.scalar
        def _(E):
            for fn in prog["scalar"]:
                fn(E)

        @block.tensor
        def _(E):
            for fn in prog["tensor"]:
                fn(E)

    return nc


_NC_CACHE = []


DBG_OUT = None


def kernel(L0_attn_hidden_states, L0_attn_Wq, L0_attn_Wk, L0_attn_Wv,
           L0_attn_Wo, L0_attn_bo, L0_ffn_gate_w, L0_ffn_gate_b, L0_ffn_bias,
           L0_ffn_w1, L0_ffn_b1, L0_ffn_w2, L0_ffn_b2):
    x = np.ascontiguousarray(np.asarray(L0_attn_hidden_states, dtype=np.float32)
                             .reshape(KV_TOK, D))
    wq_h = np.ascontiguousarray(np.asarray(L0_attn_Wq, dtype=np.float32))
    wk_h = np.ascontiguousarray(np.asarray(L0_attn_Wk, dtype=np.float32))
    wv_h = np.ascontiguousarray(np.asarray(L0_attn_Wv, dtype=np.float32))
    wo_h = np.ascontiguousarray(np.asarray(L0_attn_Wo, dtype=np.float32))
    bo_h = np.ascontiguousarray(np.asarray(L0_attn_bo, dtype=np.float32)
                                .reshape(1, D))
    gate_w = np.asarray(L0_ffn_gate_w, dtype=np.float32)
    gwt_h = np.ascontiguousarray(gate_w.T)
    gate_b_h = np.ascontiguousarray(np.asarray(L0_ffn_gate_b, np.float32)
                                    .reshape(1, NE))
    bias_h = np.ascontiguousarray(np.asarray(L0_ffn_bias, np.float32)
                                  .reshape(1, NE))
    w1_h = np.asarray(L0_ffn_w1, dtype=np.float32)
    w1t_h = np.ascontiguousarray(w1_h.transpose(0, 2, 1))
    b1_h = np.ascontiguousarray(np.asarray(L0_ffn_b1, np.float32))
    w2_h = np.asarray(L0_ffn_w2, dtype=np.float32)
    w2t_h = np.ascontiguousarray(w2_h.transpose(0, 2, 1))
    b2_h = np.ascontiguousarray(np.asarray(L0_ffn_b2, np.float32))

    ident = np.eye(128, dtype=ml_dtypes.bfloat16)
    ones_row = np.ones((1, 128), dtype=ml_dtypes.bfloat16)
    ones_col = np.ones((128, 1), dtype=ml_dtypes.bfloat16)
    sel = np.zeros((32, D), dtype=ml_dtypes.bfloat16)
    for p in range(NPAIR):
        sel[2 * p, 128 * p:128 * p + 64] = 1
        sel[2 * p + 1, 128 * p + 64:128 * p + 128] = 1
    sel8 = np.zeros((NE, NE * 128), dtype=ml_dtypes.bfloat16)
    for e in range(NE):
        sel8[e, 128 * e:128 * e + 128] = 1

    if not _NC_CACHE:
        _NC_CACHE.append(build_nc())
    nc = _NC_CACHE[0]

    common = dict(xkv=x, wq=wq_h, wk=wk_h, wv=wv_h, wo=wo_h, bo=bo_h,
                  gwt=gwt_h, gate_b=gate_b_h, bias_e=bias_h, w1t=w1t_h,
                  b1=b1_h, w2t=w2t_h, b2=b2_h, ident=ident,
                  ones_row=ones_row, ones_col=ones_col, sel=sel, sel8=sel8)
    in_maps = []
    for c in range(NCORES):
        m = dict(common)
        m["xs"] = np.ascontiguousarray(x[TOK * c:TOK * c + TOK, :])
        in_maps.append(m)

    res = run_bass_kernel_spmd(nc, in_maps, core_ids=list(range(NCORES)))
    if DBG is not None:
        global DBG_OUT
        DBG_OUT = [np.asarray(res.results[c]["dbg"]) for c in range(NCORES)]
    full = np.concatenate([np.asarray(res.results[c]["out"])
                           for c in range(NCORES)], axis=0)
    return full.reshape(1, KV_TOK, D)


# version 06
# speedup vs baseline: 1.3941x; 1.3941x over previous
import contextlib
import numpy as np
import ml_dtypes
import concourse.bass as bass
import concourse.mybir as mybir
from concourse.bass_utils import run_bass_kernel_spmd

F32 = mybir.dt.float32
BF16 = mybir.dt.bfloat16
AF = mybir.ActivationFunctionType
ALU = mybir.AluOpType
AX = mybir.AxisListType

NCORES = 8
TOK = 1024
NTT = TOK // 128
D = 2048
NDT = D // 128
NPAIR = 16
FF = 1408
NHT = 11
NE = 8
KV_TOK = 8192
KV_NTT = KV_TOK // 128
EPS = 1e-5
BIG = 1.0e9
HTG = [(0, 4), (4, 8), (8, 11)]
LIMIT = 4
SUB = 99
QH = 2
QMT = 8
QLOAD = True
QDST = 0
QWAIT = True
MOEE = NE
DBG = None


def build_nc():
    nc = bass.Bass("TRN2", target_bir_lowering=False,
                   dynamic_dma_scratch_size=16384)

    def din(name, shape, dt=F32):
        return nc.dram_tensor(name, shape, dt, kind="ExternalInput").ap()

    xs = din("xs", [TOK, D])
    xkv = din("xkv", [KV_TOK, D])
    wq = din("wq", [D, D])
    wk = din("wk", [D, D])
    wv = din("wv", [D, D])
    wo = din("wo", [D, D])
    bo = din("bo", [1, D])
    gwt = din("gwt", [D, NE])
    gate_b = din("gate_b", [1, NE])
    bias_e = din("bias_e", [1, NE])
    w1t = din("w1t", [NE, D, FF])
    b1 = din("b1", [NE, FF])
    w2t = din("w2t", [NE, FF, D])
    b2 = din("b2", [NE, D])
    ident_in = din("ident", [128, 128], BF16)
    onesr_in = din("ones_row", [1, 128], BF16)
    onesc_in = din("ones_col", [128, 1], BF16)
    sel_in = din("sel", [32, D], BF16)
    sel8_in = din("sel8", [NE, NE * 128], BF16)

    out = nc.dram_tensor("out", [TOK, D], F32, kind="ExternalOutput").ap()
    wqb = nc.dram_tensor("wqb", [D, D], BF16, kind="Internal").ap()
    wob = nc.dram_tensor("wob", [D, D], BF16, kind="Internal").ap()
    w1b = nc.dram_tensor("w1b", [NE, D, FF], BF16, kind="Internal").ap()
    w2b = nc.dram_tensor("w2b", [NE, FF, D], BF16, kind="Internal").ap()

    AR1 = nc.alloc_sbuf_tensor("AR1", [128, 16384], F32)
    AR2 = nc.alloc_sbuf_tensor("AR2", [128, 16384], BF16)
    AR3 = nc.alloc_sbuf_tensor("AR3", [128, 16384], BF16)
    AR4 = nc.alloc_sbuf_tensor("AR4", [128, 16384], BF16)

    id_sb = nc.alloc_sbuf_tensor("id_sb", [128, 128], BF16)
    onesr = nc.alloc_sbuf_tensor("onesr", [1, 128], BF16)
    onesc = nc.alloc_sbuf_tensor("onesc", [128, 1], BF16)
    sel_sb = nc.alloc_sbuf_tensor("sel_sb", [32, D], BF16)
    sel8_sb = nc.alloc_sbuf_tensor("sel8_sb", [NE, NE * 128], BF16)
    bo_sb = nc.alloc_sbuf_tensor("bo_sb", [1, D], BF16)
    b2_sb = nc.alloc_sbuf_tensor("b2_sb", [NE, D], BF16)
    gwt_sb = nc.alloc_sbuf_tensor("gwt_sb", [128, NDT * NE], BF16)
    gbb_f = nc.alloc_sbuf_tensor("gbb_f", [1, 2 * NE], F32)
    gbb_sb = nc.alloc_sbuf_tensor("gbb_sb", [1, NE], BF16)
    b1row = nc.alloc_sbuf_tensor("b1row", [1, FF], BF16)
    b1col = nc.alloc_sbuf_tensor("b1col", [128, NHT], F32)
    kvacc = nc.alloc_sbuf_tensor("kvacc", [128, NPAIR * 128], F32)
    ksacc = nc.alloc_sbuf_tensor("ksacc", [128, NPAIR], F32)
    kvl = nc.alloc_sbuf_tensor("kvl", [128, NPAIR * 128], BF16)
    ks_sb = nc.alloc_sbuf_tensor("ks_sb", [128, NPAIR * 32], BF16)
    stats = nc.alloc_sbuf_tensor("stats", [128, 32], F32)
    rcp_f = nc.alloc_sbuf_tensor("rcp_f", [128, 32], F32)
    rcp_bf = nc.alloc_sbuf_tensor("rcp_bf", [128, NTT * 32], BF16)
    rcpT = nc.alloc_sbuf_tensor("rcpT", [32, TOK], BF16)
    crep_sb = nc.alloc_sbuf_tensor("crep_sb", [128, 1024], F32)
    comb_f = nc.alloc_sbuf_tensor("comb_f", [128, NTT * NE], F32)
    comb_bf = nc.alloc_sbuf_tensor("comb_bf", [128, NTT * NE], BF16)
    combT = nc.alloc_sbuf_tensor("combT", [NE, TOK], BF16)
    cscr = nc.alloc_sbuf_tensor("cscr", [128, 4 * NE], F32)
    knat = nc.alloc_sbuf_tensor("knat", [128, D], BF16)

    PS0 = nc.alloc_psum_tensor("PS0", [128, 2048], F32)
    PS1 = nc.alloc_psum_tensor("PS1", [128, 2048], F32)

    ctx = contextlib.ExitStack()
    sems = {}
    cnt = {}

    def getsem(name):
        if name not in sems:
            sems[name] = ctx.enter_context(nc.semaphore("sem_" + name))
            cnt[name] = 0
        return sems[name]

    for name in ("v", "a", "t", "o"):
        getsem(name)

    prog = {e: [] for e in ("sync", "gpsimd", "vector", "scalar", "tensor")}

    def emit(eng, fn):
        prog[eng].append(fn)

    def plan(sem, n=1):
        getsem(sem)
        cnt[sem] += n
        return cnt[sem]

    def wait(eng, sem, val):
        if val <= 0:
            return
        getsem(sem)
        assert val <= cnt[sem], (eng, sem, val, cnt[sem])
        if sem not in ("v", "a", "t"):
            assert val == cnt[sem] or val <= 0, (eng, sem, val, cnt[sem])
        if val <= 0:
            return
        h = sems[sem]
        emit(eng, lambda E, h=h, v=val: E.wait_ge(h, v))

    def run(eng, sem, fn, n=1):
        getsem(sem)
        v = plan(sem, n)
        h = sems[sem]
        emit(eng, lambda E, h=h, n=n, fn=fn: fn(E).then_inc(h, n))
        return v

    OWNER = {"t": "tensor", "v": "vector", "a": "scalar"}

    def gate_gpsimd(*conds):
        getsem("gg")
        for sem, val in conds:
            if val <= 0:
                continue
            assert sem in OWNER
            assert val <= cnt[sem]
            eng = OWNER[sem]
            hown = sems[sem]
            emit(eng, lambda E, hown=hown, v=val: E.wait_ge(hown, v))
            hgg = sems["gg"]
            plan("gg", 1)
            emit(eng, lambda E, hgg=hgg: E.nop().then_inc(hgg, 1))
        wait("gpsimd", "gg", cnt["gg"])

    wk_res = AR1[:].bitcast(BF16)

    def wv_chunk(dt):
        if dt < 8:
            return AR2[:, 2048 * dt:2048 * dt + 2048]
        return AR3[:, 2048 * (dt - 8):2048 * (dt - 8) + 2048]

    def x_tile(tt):
        return AR1[:, 2048 * tt:2048 * tt + 2048]

    def lnT_dt(dt):
        return AR2[:, 1024 * dt:1024 * dt + 1024]

    def qT_pair(p):
        return AR3[:, 1024 * p:1024 * p + 1024]

    def hs_ht(ht):
        return AR3[:, 1024 * ht:1024 * ht + 1024]

    ar4f = AR4[:].bitcast(F32)
    vnat = crep_sb[:].bitcast(BF16)

    def xk_slot(r):
        return ar4f[:, 2048 * r:2048 * r + 2048]

    def lnN_kv_slot(r):
        return AR4[:, 8192 + 2048 * r:8192 + 2048 * r + 2048]

    def lnTt_slot(r):
        return AR4[:, 12288 + 2048 * r:12288 + 2048 * r + 2048]

    def wq_half_dt(dt):
        if QDST == 1:
            return AR3[:, 1024 * (dt % 4) + 4096:1024 * (dt % 4) + 5120]
        return AR4[:, 1024 * dt:1024 * dt + 1024]

    def wo_slot(r):
        return AR4[:, 2048 * r:2048 * r + 2048]

    def w1_slot(r):
        return AR4[:, 1408 * r:1408 * r + 1408]

    def w2_slot(r):
        return AR4[:, 2816 + 2048 * r:2816 + 2048 * r + 2048]

    def silu_slot(ssl):
        return ar4f[:, 5504 + 1024 * ssl:5504 + 1024 * ssl + 1024]

    def lnN_own_slot(r):
        return knat[:] if r == 0 else vnat

    def bankv(ps, b, n=1):
        return ps[:, 512 * b:512 * b + 512 * n]

    run("sync", "sc", lambda E: E.dma_start(out=id_sb[:], in_=ident_in), 16)
    run("sync", "sc", lambda E: E.dma_start(out=onesr[:], in_=onesr_in), 16)
    run("sync", "sc", lambda E: E.dma_start(out=onesc[:], in_=onesc_in), 16)
    run("sync", "sc", lambda E: E.dma_start(out=sel_sb[:], in_=sel_in), 16)
    run("sync", "sc", lambda E: E.dma_start(out=sel8_sb[:], in_=sel8_in), 16)
    run("sync", "sc", lambda E: E.dma_start(out=gbb_f[:, 0:NE], in_=gate_b), 16)
    s_consts = run("sync", "sc",
                   lambda E: E.dma_start(out=gbb_f[:, NE:2 * NE], in_=bias_e), 16)

    run("gpsimd", "ss", lambda E: E.dma_start(out=bo_sb[:], in_=bo), 16)
    run("gpsimd", "ss", lambda E: E.dma_start(out=b2_sb[:], in_=b2), 16)
    g_small = run("gpsimd", "ss", lambda E: E.dma_start(
        out=gwt_sb[:].rearrange("p (dt e) -> p dt e", dt=NDT),
        in_=gwt.rearrange("(dt p) e -> p dt e", p=128)), 16)

    wait("vector", "sc", s_consts)
    run("vector", "v", lambda E: E.tensor_add(
        gbb_f[:, 0:NE], gbb_f[:, 0:NE], gbb_f[:, NE:2 * NE]))
    wait("vector", "v", cnt["v"])
    v_gbb = run("vector", "v", lambda E: E.tensor_copy(gbb_sb[:], gbb_f[:, 0:NE]))

    run("vector", "v", lambda E: E.memset(kvl[:], 0.0))
    v_zero = run("vector", "v", lambda E: E.memset(ks_sb[:], 0.0))

    def flat(ap):
        r = ap
        while len(r.shape) > 1:
            r = r.rearrange("a b -> (a b)") if len(r.shape) == 2 else                 r.rearrange("a b c -> (a b) c")
        return r[None, :]

    run("gpsimd", "sbwq", lambda E: E.dma_start(out=flat(wqb), in_=flat(wq)), 16)
    run("gpsimd", "sbwo", lambda E: E.dma_start(out=flat(wob), in_=flat(wo)), 16)
    for e in range(NE):
        run("gpsimd", "sbw1", lambda E, e=e: E.dma_start(
            out=flat(w1b[e]), in_=flat(w1t[e])), 16)
        run("gpsimd", "sbw2", lambda E, e=e: E.dma_start(
            out=flat(w2b[e]), in_=flat(w2t[e])), 16)
    for dt in range(NDT):
        run("gpsimd", "swkv", lambda E, dt=dt: E.dma_start(
            out=wk_res[:, 2048 * dt:2048 * dt + 2048],
            in_=wk[128 * dt:128 * dt + 128, :]), 16)
    for dt in range(NDT):
        run("gpsimd", "swkv", lambda E, dt=dt: E.dma_start(
            out=wv_chunk(dt), in_=wv[128 * dt:128 * dt + 128, :]), 16)
    g_wkv = cnt["swkv"]

    ln_prev = {"v": 0, "a": 0}

    def emit_ln_tile(src_ap, nat_slot, dst_writer):
        for c in range(4):
            run("vector", "v", lambda E, c=c: E.bn_stats(
                stats[:, 6 * c:6 * c + 6], src_ap[:, 512 * c:512 * c + 512]))
        wait("vector", "v", cnt["v"])
        run("vector", "v", lambda E: E.bn_aggr(stats[:, 24:26], stats[:, 0:24]))
        v_mv = cnt["v"]
        wait("vector", "v", v_mv)
        run("vector", "v", lambda E: E.tensor_scalar_add(
            stats[:, 30:31], stats[:, 25:26], EPS))
        v_ve = cnt["v"]
        wait("scalar", "v", v_ve)
        run("scalar", "a", lambda E: E.activation(
            stats[:, 26:27], stats[:, 30:31], AF.Sqrt))
        wait("vector", "a", cnt["a"])
        run("vector", "v", lambda E: E.reciprocal(stats[:, 27:28], stats[:, 26:27]))
        wait("vector", "v", cnt["v"])
        run("vector", "v", lambda E: E.tensor_mul(
            stats[:, 28:29], stats[:, 24:25], stats[:, 27:28]))
        wait("vector", "v", cnt["v"])
        v_ab = run("vector", "v", lambda E: E.tensor_scalar_mul(
            stats[:, 29:30], stats[:, 28:29], -1.0))
        wait("scalar", "v", v_ab)
        a_ln = run("scalar", "a", lambda E: E.activation(
            nat_slot, src_ap, AF.Identity,
            bias=stats[:, 29:30], scale=stats[:, 27:28]))
        wait("tensor", "a", a_ln)
        for j in range(4):
            ps = bankv(PS0, 3) if j % 2 == 0 else bankv(PS1, 3)
            eng = "vector" if j % 2 == 0 else "scalar"
            ek = "v" if j % 2 == 0 else "a"
            wait("tensor", ek, ln_prev[ek])
            for u in range(4):
                dtj = 4 * j + u
                run("tensor", "t", lambda E, ps=ps, u=u, dtj=dtj: E.matmul(
                    ps[:, 128 * u:128 * u + 128],
                    nat_slot[:, 128 * dtj:128 * dtj + 128],
                    id_sb[:], start=True, stop=True))
            t_pack = cnt["t"]
            wait(eng, "t", t_pack)
            dst_writer(j, ps, eng)
            ln_prev[ek] = cnt[ek]
        return a_ln

    xk_lnread_a = {}
    lnTt_done_t = {}
    kv_add_done = {}

    kv_range = range(KV_NTT if LIMIT >= 1 else 0)
    for kt in kv_range:
        r = kt % 2
        if kt >= 2:
            wait("sync", "a", xk_lnread_a[kt - 2])
        sxn = "sx%d" % r
        s_x = run("sync", sxn, lambda E, kt=kt, r=r: E.dma_start(
            out=xk_slot(r), in_=xkv[128 * kt:128 * kt + 128, :]), 16)
        wait("vector", sxn, s_x)
        wait("scalar", sxn, s_x)
        if kt >= 2:
            tprev = lnTt_done_t[kt - 2]
            wait("vector", "t", tprev)
            wait("scalar", "t", tprev)
        lt = kt % 2

        def lnTt_writer(j, ps, eng, lt=lt):
            dst = lnTt_slot(lt)[:, 512 * j:512 * j + 512]
            if eng == "vector":
                run("vector", "v", lambda E, ps=ps, dst=dst: E.tensor_copy(dst, ps))
            else:
                run("scalar", "a", lambda E, ps=ps, dst=dst: E.copy(dst, ps))

        a_ln = emit_ln_tile(xk_slot(r), lnN_kv_slot(lt), lnTt_writer)
        xk_lnread_a[kt] = a_ln
        v_lnTt = cnt["v"]
        a_lnTt = cnt["a"]

        if kt == 0:
            wait("tensor", "sc", s_consts)
            wait("tensor", "swkv", g_wkv)
        if kt >= 1:
            wait("tensor", "v", kv_add_done[kt - 1])
        wait("tensor", "v", v_lnTt)
        wait("tensor", "a", a_lnTt)
        for dt in range(NDT):
            lhs = lnTt_slot(lt)[:, 128 * dt:128 * dt + 128]
            for c in range(4):
                run("tensor", "t", lambda E, lhs=lhs, dt=dt, c=c: E.matmul(
                    bankv(PS0, c), lhs,
                    wk_res[:, 2048 * dt + 512 * c:2048 * dt + 512 * c + 512],
                    start=(dt == 0), stop=(dt == NDT - 1)))
            for c in range(4):
                run("tensor", "t", lambda E, lhs=lhs, dt=dt, c=c: E.matmul(
                    bankv(PS1, c), lhs,
                    wv_chunk(dt)[:, 512 * c:512 * c + 512],
                    start=(dt == 0), stop=(dt == NDT - 1)))
        t_proj = cnt["t"]
        lnTt_done_t[kt] = t_proj

        wait("scalar", "t", t_proj)
        a_k = run("scalar", "a", lambda E: E.activation(knat[:], PS0[:], AF.Relu))
        wait("vector", "t", t_proj)
        v_v = run("vector", "v", lambda E: E.tensor_copy(vnat, PS1[:]))

        wait("tensor", "a", a_k)
        wait("tensor", "v", v_v)
        for p in range(NPAIR):
            ps = PS0 if p < 8 else PS1
            q = p % 8
            run("tensor", "t", lambda E, ps=ps, p=p, q=q: E.matmul(
                ps[:, 128 * q:128 * q + 128],
                knat[:, 128 * p:128 * p + 128],
                vnat[:, 128 * p:128 * p + 128], start=True, stop=True))
            run("tensor", "t", lambda E, ps=ps, p=p, q=q: E.matmul(
                ps[:, 1024 + q:1024 + q + 1],
                knat[:, 128 * p:128 * p + 128],
                onesc[:], start=True, stop=True))
        t_kv = cnt["t"]

        wait("vector", "t", t_kv)
        if kt == 0:
            run("vector", "v", lambda E: E.tensor_copy(kvacc[:, 0:1024],
                                                       PS0[:, 0:1024]))
            run("vector", "v", lambda E: E.tensor_copy(kvacc[:, 1024:2048],
                                                       PS1[:, 0:1024]))
            run("vector", "v", lambda E: E.tensor_copy(ksacc[:, 0:8],
                                                       PS0[:, 1024:1032]))
            run("vector", "v", lambda E: E.tensor_copy(ksacc[:, 8:16],
                                                       PS1[:, 1024:1032]))
        else:
            run("vector", "v", lambda E: E.tensor_add(
                kvacc[:, 0:1024], PS0[:, 0:1024], kvacc[:, 0:1024]))
            run("vector", "v", lambda E: E.tensor_add(
                kvacc[:, 1024:2048], PS1[:, 0:1024], kvacc[:, 1024:2048]))
            run("vector", "v", lambda E: E.tensor_add(
                ksacc[:, 0:8], PS0[:, 1024:1032], ksacc[:, 0:8]))
            run("vector", "v", lambda E: E.tensor_add(
                ksacc[:, 8:16], PS1[:, 1024:1032], ksacc[:, 8:16]))
        kv_add_done[kt] = cnt["v"]

    t_kv_final = cnt["t"]
    v_kv_final = cnt["v"]

    wait("vector", "v", v_kv_final)
    wait("vector", "v", v_zero)
    for p in range(NPAIR if LIMIT >= 1 else 0):
        c0 = 128 * p
        run("vector", "v", lambda E, c0=c0: E.tensor_copy(
            kvl[0:64, c0:c0 + 64], kvacc[0:64, c0:c0 + 64]))
        run("vector", "v", lambda E, c0=c0: E.tensor_copy(
            kvl[64:128, c0 + 64:c0 + 128], kvacc[64:128, c0 + 64:c0 + 128]))
        k0 = 32 * p
        run("vector", "v", lambda E, p=p, k0=k0: E.tensor_copy(
            ks_sb[0:64, k0 + 2 * p:k0 + 2 * p + 1], ksacc[0:64, p:p + 1]))
        run("vector", "v", lambda E, p=p, k0=k0: E.tensor_copy(
            ks_sb[64:128, k0 + 2 * p + 1:k0 + 2 * p + 2],
            ksacc[64:128, p:p + 1]))
    v_kvl = cnt["v"]

    wait("sync", "t", t_kv_final)
    s_xs = run("sync", "sxs", lambda E: E.dma_start(
        out=AR1[:].rearrange("p (tt d) -> p tt d", tt=NTT),
        in_=xs.rearrange("(tt p) d -> p tt d", p=128)), 16)

    wait("vector", "t", t_kv_final)
    wait("scalar", "t", t_kv_final)
    wait("vector", "sxs", s_xs)
    wait("scalar", "sxs", s_xs)
    x_res_done = {tt: 0 for tt in range(NTT)}
    ln_marks = {}

    def emit_own_ln(tt, prev_mark):
        lt = tt % 2

        def wtr(j, ps, eng, tt=tt):
            for u in range(4):
                dtj = 4 * j + u
                dst = lnT_dt(dtj)[:, 128 * tt:128 * tt + 128]
                src = ps[:, 128 * u:128 * u + 128]
                if eng == "vector":
                    run("vector", "v", lambda E, s=src, d=dst: E.tensor_copy(d, s))
                else:
                    run("scalar", "a", lambda E, s=src, d=dst: E.copy(d, s))

        if prev_mark is not None:
            wait("vector", "t", prev_mark)
            wait("scalar", "t", prev_mark)
        emit_ln_tile(x_tile(tt), lnN_own_slot(lt), wtr)
        return cnt["t"]

    for tt in range(NTT if LIMIT >= 2 else 0):
        ln_marks[tt] = emit_own_ln(tt, ln_marks.get(tt - 2))
    v_ln1 = cnt["v"]
    a_ln1 = cnt["a"]
    if LIMIT < 2:
        emit_attn = False
    else:
        emit_attn = True

    class _StopAttn(Exception):
        pass

    v_x_attn = 0
    try:
        if not (emit_attn and SUB >= 1):
                raise _StopAttn()
        wait("sync", "a", xk_lnread_a.get(KV_NTT - 1, 0))
        wait("sync", "t", t_kv_final)
        q_evac_a = {}
        for h in range(QH):
            if h == 0:
                wait("sync", "sbwq", 16)
            if h == 1:
                wait("sync", "t", cnt["t"])
            if QLOAD:
                for dt in range(NDT):
                    run("sync", "swq", lambda E, h=h, dt=dt: E.dma_start(
                        out=wq_half_dt(dt),
                        in_=wqb[128 * dt:128 * dt + 128,
                                1024 * h:1024 * h + 1024]), 16)
            g_wq = cnt.get("swq", 0)
            if g_wq:
                wait("tensor", "swq", g_wq)
            if h == 0:
                wait("tensor", "v", v_ln1)
                wait("tensor", "a", a_ln1)
            for m in range(QMT):
                pg = 8 * h + m
                ps = PS0 if pg % 2 == 0 else PS1
                if pg >= 2:
                    wait("tensor", "a", q_evac_a[pg - 2])
                for dt in range(NDT):
                    lhs = wq_half_dt(dt)[:, 128 * m:128 * m + 128]
                    for c in range(2):
                        run("tensor", "t", lambda E, ps=ps, lhs=lhs, dt=dt, c=c:
                            E.matmul(bankv(ps, c), lhs,
                                     lnT_dt(dt)[:, 512 * c:512 * c + 512],
                                     start=(dt == 0), stop=(dt == NDT - 1)))
                t_q = cnt["t"]
                wait("scalar", "t", t_q)
                q_evac_a[pg] = run("scalar", "a", lambda E, ps=ps, pg=pg:
                                   E.activation(qT_pair(pg), ps[:, 0:1024], AF.Relu))
        a_q = cnt["a"]

        if SUB < 2:
            raise _StopAttn()
        wait("tensor", "a", a_q)
        wait("tensor", "v", v_kvl)
        for tt in range(NTT):
            for p in range(NPAIR):
                run("tensor", "t", lambda E, tt=tt, p=p: E.matmul(
                    PS1[:, 1536 + 32 * tt:1536 + 32 * tt + 32],
                    qT_pair(p)[:, 128 * tt:128 * tt + 128],
                    ks_sb[:, 32 * p:32 * p + 32],
                    start=(p == 0), stop=(p == NPAIR - 1)))
        t_norm = cnt["t"]

        wait("vector", "t", t_norm)
        for tt in range(NTT):
            wait("vector", "v", cnt["v"])
            run("vector", "v", lambda E, tt=tt: E.tensor_scalar_max(
                rcp_f[:], PS1[:, 1536 + 32 * tt:1536 + 32 * tt + 32], 1e-6))
            wait("vector", "v", cnt["v"])
            run("vector", "v", lambda E: E.reciprocal(rcp_f[:], rcp_f[:]))
            wait("vector", "v", cnt["v"])
            run("vector", "v", lambda E, tt=tt: E.tensor_copy(
                rcp_bf[:, 32 * tt:32 * tt + 32], rcp_f[:]))
        v_rcp = cnt["v"]

        wait("tensor", "v", v_rcp)
        for tt in range(NTT):
            run("tensor", "t", lambda E, tt=tt: E.matmul(
                PS1[0:32, 1536:1664], rcp_bf[:, 32 * tt:32 * tt + 32],
                id_sb[:], start=True, stop=True))
            t_r = cnt["t"]
            wait("vector", "t", t_r)
            v_r = run("vector", "v", lambda E, tt=tt: E.tensor_copy(
                rcpT[0:32, 128 * tt:128 * tt + 128], PS1[0:32, 1536:1664]))
            wait("tensor", "v", v_r)
        v_rcpT = cnt["v"]

        if SUB < 3:
            raise _StopAttn()
        attnT_v = {}
        rep_a = {}
        for p in range(NPAIR):
            ps_q = PS0 if p % 2 == 0 else PS1
            if p >= 2:
                wait("tensor", "v", attnT_v[p - 2])
            for c in range(2):
                run("tensor", "t", lambda E, ps_q=ps_q, p=p, c=c: E.matmul(
                    bankv(ps_q, c), kvl[:, 128 * p:128 * p + 128],
                    qT_pair(p)[:, 512 * c:512 * c + 512], start=True, stop=True))
            for c in range(2):
                run("tensor", "t", lambda E, ps_q=ps_q, p=p, c=c: E.matmul(
                    ps_q[:, 1024 + 512 * c:1024 + 512 * c + 512],
                    sel_sb[:, 128 * p:128 * p + 128],
                    rcpT[0:32, 512 * c:512 * c + 512], start=True, stop=True))
            t_p = cnt["t"]
            wait("scalar", "t", t_p)
            if p >= 1:
                wait("scalar", "v", attnT_v[p - 1])
            rep_a[p] = run("scalar", "a", lambda E, ps_q=ps_q: E.copy(
                ar4f[:, 0:1024], ps_q[:, 1024:2048]))
            wait("vector", "t", t_p)
            wait("vector", "a", rep_a[p])
            attnT_v[p] = run("vector", "v", lambda E, ps_q=ps_q, p=p: E.tensor_mul(
                qT_pair(p), ps_q[:, 0:1024], ar4f[:, 0:1024]))
        v_attnT = cnt["v"]

        if SUB < 4:
            raise _StopAttn()
        wait("sync", "t", cnt["t"])
        wait("sync", "v", v_attnT)
        wait("sync", "sbwo", 16)
        x_res_done = {}
        wo_cons = {}
        wait("tensor", "v", v_attnT)
        for g in range(4):
            if g >= 1:
                wait("tensor", "v", x_res_done[2 * g - 1])
            wait("tensor", "ss", g_small)
            for u in range(2):
                ps = PS0 if u == 0 else PS1
                for c in range(4):
                    run("tensor", "t", lambda E, ps=ps, c=c: E.matmul(
                        bankv(ps, c), onesr[0:1, :],
                        bo_sb[0:1, 512 * c:512 * c + 512], start=True, stop=False))
            for p in range(NPAIR):
                idx = g * NPAIR + p
                pm = wo_cons.get(idx - 4)
                if pm is not None:
                    wait("sync", "t", pm)
                swon = "swo%d" % (p % 4)
                gm = run("sync", swon, lambda E, p=p: E.dma_start(
                    out=wo_slot(p % 4), in_=wob[128 * p:128 * p + 128, :]), 16)
                wait("tensor", swon, gm)
                for u in range(2):
                    tt = 2 * g + u
                    ps = PS0 if u == 0 else PS1
                    lhs = qT_pair(p)[:, 128 * tt:128 * tt + 128]
                    last = (p == NPAIR - 1)
                    for c in range(4):
                        run("tensor", "t",
                            lambda E, ps=ps, lhs=lhs, p=p, c=c, last=last:
                            E.matmul(bankv(ps, c), lhs,
                                     wo_slot(p % 4)[:, 512 * c:512 * c + 512],
                                     start=False, stop=last))
                wo_cons[idx] = cnt["t"]
            t_g = cnt["t"]
            for u in range(2):
                tt = 2 * g + u
                ps = PS0 if u == 0 else PS1
                wait("vector", "t", t_g)
                wait("vector", "sxs", s_xs)
                x_res_done[tt] = run("vector", "v", lambda E, ps=ps, tt=tt:
                                     E.tensor_add(x_tile(tt), ps[:], x_tile(tt)))
        v_x_attn = cnt["v"]
    except _StopAttn:
        pass

    wait("tensor", "v", v_x_attn)
    ln2_marks = {}
    for tt in range(NTT if LIMIT >= 3 else 0):
        lt = tt % 2

        def wtr2(j, ps, eng, tt=tt):
            for u in range(4):
                dtj = 4 * j + u
                dst = lnT_dt(dtj)[:, 128 * tt:128 * tt + 128]
                src = ps[:, 128 * u:128 * u + 128]
                if eng == "vector":
                    run("vector", "v", lambda E, s=src, d=dst: E.tensor_copy(d, s))
                else:
                    run("scalar", "a", lambda E, s=src, d=dst: E.copy(d, s))

        wait("vector", "v", x_res_done[tt])
        wait("scalar", "v", x_res_done[tt])
        pm = ln2_marks.get(tt - 2)
        if pm is not None:
            wait("vector", "t", pm)
            wait("scalar", "t", pm)
        emit_ln_tile(x_tile(tt), lnN_own_slot(lt), wtr2)
        ln2_marks[tt] = cnt["t"]
    v_ln2 = cnt["v"]
    a_ln2 = cnt["a"]

    wait("tensor", "v", v_ln2)
    wait("tensor", "a", a_ln2)
    wait("tensor", "v", v_gbb)
    comb_read_v = {}
    for tt in range(NTT if LIMIT >= 3 else 0):
        if tt >= 1:
            wait("tensor", "v", comb_read_v[tt - 1])
        for dt in range(NDT):
            run("tensor", "t", lambda E, tt=tt, dt=dt: E.matmul(
                PS0[:, 8 * tt:8 * tt + 8],
                lnT_dt(dt)[:, 128 * tt:128 * tt + 128],
                gwt_sb[:, 8 * dt:8 * dt + 8],
                start=(dt == 0), stop=False))
        run("tensor", "t", lambda E, tt=tt: E.matmul(
            PS0[:, 8 * tt:8 * tt + 8], onesr[0:1, :],
            gbb_sb[0:1, :], start=False, stop=True))
        t_sc = cnt["t"]
        sc = PS0[:, 8 * tt:8 * tt + 8]
        m1 = cscr[:, 0:1]
        m2 = cscr[:, 1:2]
        g1 = cscr[:, 2:3]
        g2 = cscr[:, 3:4]
        is1 = cscr[:, 8:16]
        msk = cscr[:, 16:24]
        t2 = cscr[:, 24:32]
        wait("vector", "t", t_sc)
        run("vector", "v", lambda E, sc=sc, m1=m1: E.reduce_max(m1, sc, axis=AX.X))
        wait("vector", "v", cnt["v"])
        run("vector", "v", lambda E, sc=sc, is1=is1, m1=m1: E.tensor_scalar(
            is1, sc, m1, None, ALU.is_ge))
        wait("vector", "v", cnt["v"])
        comb_read_v[tt] = run("vector", "v",
                              lambda E, sc=sc, is1=is1, msk=msk:
                              E.scalar_tensor_tensor(msk, is1, -BIG, sc,
                                                     ALU.mult, ALU.add))
        wait("vector", "v", cnt["v"])
        run("vector", "v", lambda E, msk=msk, m2=m2: E.reduce_max(
            m2, msk, axis=AX.X))
        wait("vector", "v", cnt["v"])
        v_d = run("vector", "v", lambda E, m1=m1, m2=m2, g1=g1:
                  E.tensor_sub(g1, m1, m2))
        wait("scalar", "v", v_d)
        a_g1 = run("scalar", "a", lambda E, g1=g1: E.activation(
            g1, g1, AF.Sigmoid))
        wait("vector", "a", a_g1)
        run("vector", "v", lambda E, g1=g1, g2=g2: E.tensor_scalar(
            g2, g1, -1.0, 1.0, ALU.mult, ALU.add))
        wait("vector", "v", cnt["v"])
        run("vector", "v", lambda E, is1=is1, g1=g1: E.tensor_scalar(
            is1, is1, g1, None, ALU.mult))
        run("vector", "v", lambda E, msk=msk, m2=m2, t2=t2: E.tensor_scalar(
            t2, msk, m2, None, ALU.is_ge))
        wait("vector", "v", cnt["v"])
        run("vector", "v", lambda E, t2=t2, g2=g2: E.tensor_scalar(
            t2, t2, g2, None, ALU.mult))
        wait("vector", "v", cnt["v"])
        run("vector", "v", lambda E, tt=tt, is1=is1, t2=t2: E.tensor_add(
            comb_f[:, 8 * tt:8 * tt + 8], is1, t2))
        wait("vector", "v", cnt["v"])
        run("vector", "v", lambda E, tt=tt: E.tensor_copy(
            comb_bf[:, 8 * tt:8 * tt + 8], comb_f[:, 8 * tt:8 * tt + 8]))
    v_comb = cnt["v"]

    wait("tensor", "v", v_comb)
    for tt in range(NTT if LIMIT >= 3 else 0):
        run("tensor", "t", lambda E, tt=tt: E.matmul(
            PS1[0:8, 0:128], comb_bf[:, 8 * tt:8 * tt + 8], id_sb[:],
            start=True, stop=True))
        t_c = cnt["t"]
        wait("vector", "t", t_c)
        v_c = run("vector", "v", lambda E, tt=tt: E.tensor_copy(
            combT[0:8, 128 * tt:128 * tt + 128], PS1[0:8, 0:128]))
        wait("tensor", "v", v_c)
    v_combT = cnt["v"]

    wait("sync", "t", cnt["t"])
    wait("sync", "sbw1", 16 * NE)
    wait("sync", "sbw2", 16 * NE)
    wait("vector", "a", a_ln2)

    w1_ring_mark = {}
    w2_ring_mark = {}
    x_moe_done = dict(x_res_done)
    prev_hs_v = None
    b1_read_t = None
    silu_mark = {}
    for e in range(MOEE if LIMIT >= 4 else 0):
        wait("tensor", "v", cnt["v"])
        if b1_read_t is not None:
            gate_gpsimd(("t", b1_read_t))
        g_b1 = run("gpsimd", "sb1", lambda E, e=e: E.dma_start(
            out=b1row[:], in_=b1[e:e + 1, :]), 16)

        run("tensor", "t", lambda E, e=e: E.matmul(
            PS0[:, 0:512], sel8_sb[:, 128 * e:128 * e + 128],
            combT[0:8, 0:512], start=True, stop=True))
        run("tensor", "t", lambda E, e=e: E.matmul(
            PS0[:, 512:1024], sel8_sb[:, 128 * e:128 * e + 128],
            combT[0:8, 512:1024], start=True, stop=True))
        t_rep = cnt["t"]
        wait("scalar", "t", t_rep)
        if prev_hs_v is not None:
            wait("scalar", "v", prev_hs_v)
        a_crep = run("scalar", "a", lambda E: E.copy(crep_sb[:], PS0[:, 0:1024]))

        wait("tensor", "sb1", g_b1)
        for ht in range(NHT):
            run("tensor", "t", lambda E, ht=ht: E.matmul(
                PS1[:, ht:ht + 1], b1row[0:1, 128 * ht:128 * ht + 128],
                onesc[0:1, 0:1], start=True, stop=True))
        b1_read_t = cnt["t"]
        wait("vector", "t", b1_read_t)
        v_b1 = run("vector", "v", lambda E: E.tensor_copy(b1col[:], PS1[:, 0:NHT]))

        wait("tensor", "a", a_crep)
        wait("tensor", "v", v_b1)
        hs_v = {}
        for gi, (h0, h1) in enumerate(HTG):
            nsl = h1 - h0
            slots = [(PS0, 0), (PS0, 2), (PS1, 0), (PS1, 2)][:nsl]
            for dt in range(NDT):
                i = (e * len(HTG) + gi) * NDT + dt
                slot = i % 2
                pm = w1_ring_mark.get(i - 2)
                if pm is not None:
                    wait("sync", "t", pm)
                sw1n = "sw1_%d" % slot
                gm = run("sync", sw1n, lambda E, e=e, dt=dt, slot=slot:
                         E.dma_start(out=w1_slot(slot),
                                     in_=w1b[e, 128 * dt:128 * dt + 128, :]), 16)
                wait("tensor", sw1n, gm)
                for si, ht in enumerate(range(h0, h1)):
                    ps, b0 = slots[si]
                    lhs = w1_slot(slot)[:, 128 * ht:128 * ht + 128]
                    for c in range(2):
                        run("tensor", "t",
                            lambda E, ps=ps, b0=b0, lhs=lhs, dt=dt, c=c:
                            E.matmul(
                                ps[:, 512 * (b0 + c):512 * (b0 + c) + 512],
                                lhs, lnT_dt(dt)[:, 512 * c:512 * c + 512],
                                start=(dt == 0), stop=(dt == NDT - 1)))
                w1_ring_mark[i] = cnt["t"]
            t_w1g = cnt["t"]
            for si, ht in enumerate(range(h0, h1)):
                ps, b0 = slots[si]
                ssl = si % 2
                st = silu_slot(ssl)
                wait("scalar", "t", t_w1g)
                wait("scalar", "v", v_b1)
                sm = silu_mark.get(ssl)
                if sm is not None:
                    wait("scalar", "v", sm)
                a_s = run("scalar", "a", lambda E, ps=ps, b0=b0, st=st, ht=ht:
                          E.activation(st, ps[:, 512 * b0:512 * b0 + 1024],
                                       AF.Silu, bias=b1col[:, ht:ht + 1]))
                wait("vector", "a", a_s)
                hs_v[ht] = run("vector", "v", lambda E, st=st, ht=ht:
                               E.tensor_mul(hs_ht(ht), st, crep_sb[:]))
                silu_mark[ssl] = hs_v[ht]
            wait("tensor", "a", cnt["a"])
        prev_hs_v = cnt["v"]

        wait("tensor", "v", prev_hs_v)
        for ttg in range(4):
            if e > 0 or ttg > 0:
                wait("tensor", "v", cnt["v"])
            if e == 0:
                for u in range(2):
                    tt = 2 * ttg + u
                    ps = PS0 if u == 0 else PS1
                    for c in range(4):
                        run("tensor", "t", lambda E, ps=ps, tt=tt, c=c:
                            E.matmul(bankv(ps, c),
                                     combT[0:8, 128 * tt:128 * tt + 128],
                                     b2_sb[0:8, 512 * c:512 * c + 512],
                                     start=True, stop=False))
            for ht in range(NHT):
                i = (e * 4 + ttg) * NHT + ht
                slot = i % 4
                pm = w2_ring_mark.get(i - 4)
                if pm is not None:
                    wait("sync", "t", pm)
                sw2n = "sw2_%d" % slot
                gm = run("sync", sw2n, lambda E, e=e, ht=ht, slot=slot:
                         E.dma_start(out=w2_slot(slot),
                                     in_=w2b[e, 128 * ht:128 * ht + 128, :]), 16)
                wait("tensor", sw2n, gm)
                for u in range(2):
                    tt = 2 * ttg + u
                    ps = PS0 if u == 0 else PS1
                    lhs = hs_ht(ht)[:, 128 * tt:128 * tt + 128]
                    first = (e > 0 and ht == 0)
                    last = (ht == NHT - 1)
                    for c in range(4):
                        run("tensor", "t",
                            lambda E, ps=ps, lhs=lhs, slot=slot, c=c,
                            first=first, last=last:
                            E.matmul(bankv(ps, c), lhs,
                                     w2_slot(slot)[:, 512 * c:512 * c + 512],
                                     start=first, stop=last))
                w2_ring_mark[i] = cnt["t"]
            t_w2g = cnt["t"]
            for u in range(2):
                tt = 2 * ttg + u
                ps = PS0 if u == 0 else PS1
                wait("vector", "t", t_w2g)
                wait("vector", "v", x_moe_done[tt])
                x_moe_done[tt] = run("vector", "v", lambda E, ps=ps, tt=tt:
                                     E.tensor_add(x_tile(tt), ps[:],
                                                  x_tile(tt)))

    if DBG is not None:
        dbg_map = {"AR2": (AR2, BF16, 16384), "AR3": (AR3, BF16, 16384),
                   "AR1": (AR1, F32, 16384),
                   "kvacc": (kvacc, F32, NPAIR * 128),
                   "ksacc": (ksacc, F32, NPAIR),
                   "rcpT": (rcpT, BF16, TOK)}
        dt_, ddt, dcols = dbg_map[DBG]
        dbg_out = nc.dram_tensor("dbg", list(dt_.shape), ddt,
                                 kind="ExternalOutput").ap()
        wait("sync", "v", cnt["v"])
        wait("sync", "a", cnt["a"])
        wait("sync", "t", cnt["t"])
        run("sync", "o", lambda E: E.dma_start(out=dbg_out, in_=dt_[:]), 16)
    for tt in range(NTT):
        wait("sync", "v", x_moe_done[tt])
        run("sync", "o", lambda E, tt=tt: E.dma_start(
            out=out[128 * tt:128 * tt + 128, :], in_=x_tile(tt)), 16)
    wait("sync", "o", cnt["o"])

    with nc.Block() as block:
        @block.sync
        def _(E):
            for fn in prog["sync"]:
                fn(E)

        @block.gpsimd
        def _(E):
            for fn in prog["gpsimd"]:
                fn(E)

        @block.vector
        def _(E):
            for fn in prog["vector"]:
                fn(E)

        @block.scalar
        def _(E):
            for fn in prog["scalar"]:
                fn(E)

        @block.tensor
        def _(E):
            for fn in prog["tensor"]:
                fn(E)

    return nc


_NC_CACHE = []


DBG_OUT = None


def kernel(L0_attn_hidden_states, L0_attn_Wq, L0_attn_Wk, L0_attn_Wv,
           L0_attn_Wo, L0_attn_bo, L0_ffn_gate_w, L0_ffn_gate_b, L0_ffn_bias,
           L0_ffn_w1, L0_ffn_b1, L0_ffn_w2, L0_ffn_b2):
    x = np.ascontiguousarray(np.asarray(L0_attn_hidden_states, dtype=np.float32)
                             .reshape(KV_TOK, D))
    wq_h = np.ascontiguousarray(np.asarray(L0_attn_Wq, dtype=np.float32))
    wk_h = np.ascontiguousarray(np.asarray(L0_attn_Wk, dtype=np.float32))
    wv_h = np.ascontiguousarray(np.asarray(L0_attn_Wv, dtype=np.float32))
    wo_h = np.ascontiguousarray(np.asarray(L0_attn_Wo, dtype=np.float32))
    bo_h = np.ascontiguousarray(np.asarray(L0_attn_bo, dtype=np.float32)
                                .reshape(1, D))
    gate_w = np.asarray(L0_ffn_gate_w, dtype=np.float32)
    gwt_h = np.ascontiguousarray(gate_w.T)
    gate_b_h = np.ascontiguousarray(np.asarray(L0_ffn_gate_b, np.float32)
                                    .reshape(1, NE))
    bias_h = np.ascontiguousarray(np.asarray(L0_ffn_bias, np.float32)
                                  .reshape(1, NE))
    w1_h = np.asarray(L0_ffn_w1, dtype=np.float32)
    w1t_h = np.ascontiguousarray(w1_h.transpose(0, 2, 1))
    b1_h = np.ascontiguousarray(np.asarray(L0_ffn_b1, np.float32))
    w2_h = np.asarray(L0_ffn_w2, dtype=np.float32)
    w2t_h = np.ascontiguousarray(w2_h.transpose(0, 2, 1))
    b2_h = np.ascontiguousarray(np.asarray(L0_ffn_b2, np.float32))

    ident = np.eye(128, dtype=ml_dtypes.bfloat16)
    ones_row = np.ones((1, 128), dtype=ml_dtypes.bfloat16)
    ones_col = np.ones((128, 1), dtype=ml_dtypes.bfloat16)
    sel = np.zeros((32, D), dtype=ml_dtypes.bfloat16)
    for p in range(NPAIR):
        sel[2 * p, 128 * p:128 * p + 64] = 1
        sel[2 * p + 1, 128 * p + 64:128 * p + 128] = 1
    sel8 = np.zeros((NE, NE * 128), dtype=ml_dtypes.bfloat16)
    for e in range(NE):
        sel8[e, 128 * e:128 * e + 128] = 1

    if not _NC_CACHE:
        _NC_CACHE.append(build_nc())
    nc = _NC_CACHE[0]

    common = dict(xkv=x, wq=wq_h, wk=wk_h, wv=wv_h, wo=wo_h, bo=bo_h,
                  gwt=gwt_h, gate_b=gate_b_h, bias_e=bias_h, w1t=w1t_h,
                  b1=b1_h, w2t=w2t_h, b2=b2_h, ident=ident,
                  ones_row=ones_row, ones_col=ones_col, sel=sel, sel8=sel8)
    in_maps = []
    for c in range(NCORES):
        m = dict(common)
        m["xs"] = np.ascontiguousarray(x[TOK * c:TOK * c + TOK, :])
        in_maps.append(m)

    res = run_bass_kernel_spmd(nc, in_maps, core_ids=list(range(NCORES)))
    if DBG is not None:
        global DBG_OUT
        DBG_OUT = [np.asarray(res.results[c]["dbg"]) for c in range(NCORES)]
    full = np.concatenate([np.asarray(res.results[c]["out"])
                           for c in range(NCORES)], axis=0)
    return full.reshape(1, KV_TOK, D)


# version 07
# speedup vs baseline: 1.5257x; 1.0944x over previous
import contextlib
import numpy as np
import ml_dtypes
import concourse.bass as bass
import concourse.mybir as mybir
from concourse.bass_utils import run_bass_kernel_spmd

F32 = mybir.dt.float32
BF16 = mybir.dt.bfloat16
AF = mybir.ActivationFunctionType
ALU = mybir.AluOpType
AX = mybir.AxisListType

NCORES = 8
TOK = 1024
NTT = TOK // 128
D = 2048
NDT = D // 128
NPAIR = 16
FF = 1408
NHT = 11
NE = 8
KV_TOK = 8192
KV_NTT = KV_TOK // 128
EPS = 1e-5
BIG = 1.0e9
HTG = [(0, 4), (4, 8), (8, 11)]
LIMIT = 4
SUB = 99
QH = 2
QMT = 8
QLOAD = True
QDST = 0
QWAIT = True
MOEE = NE
DBG = None


def build_nc():
    nc = bass.Bass("TRN2", target_bir_lowering=False,
                   dynamic_dma_scratch_size=16384)

    def din(name, shape, dt=F32):
        return nc.dram_tensor(name, shape, dt, kind="ExternalInput").ap()

    xs = din("xs", [TOK, D])
    xkv = din("xkv", [KV_TOK, D])
    wq = din("wq", [D, D])
    wk = din("wk", [D, D])
    wv = din("wv", [D, D])
    wo = din("wo", [D, D])
    bo = din("bo", [1, D])
    gwt = din("gwt", [D, NE])
    gate_b = din("gate_b", [1, NE])
    bias_e = din("bias_e", [1, NE])
    w1t = din("w1t", [NE, D, FF])
    b1 = din("b1", [NE, FF])
    w2t = din("w2t", [NE, FF, D])
    b2 = din("b2", [NE, D])
    ident_in = din("ident", [128, 128], BF16)
    onesr_in = din("ones_row", [1, 128], BF16)
    onesc_in = din("ones_col", [128, 1], BF16)
    sel_in = din("sel", [32, D], BF16)
    sel8_in = din("sel8", [NE, NE * 128], BF16)

    out = nc.dram_tensor("out", [TOK, D], F32, kind="ExternalOutput").ap()
    wqb = nc.dram_tensor("wqb", [D, D], BF16, kind="Internal").ap()
    wob = nc.dram_tensor("wob", [D, D], BF16, kind="Internal").ap()
    w1b = nc.dram_tensor("w1b", [NE, D, FF], BF16, kind="Internal").ap()
    w2b = nc.dram_tensor("w2b", [NE, FF, D], BF16, kind="Internal").ap()

    AR1 = nc.alloc_sbuf_tensor("AR1", [128, 16384], F32)
    AR2 = nc.alloc_sbuf_tensor("AR2", [128, 16384], BF16)
    AR3 = nc.alloc_sbuf_tensor("AR3", [128, 16384], BF16)
    AR4 = nc.alloc_sbuf_tensor("AR4", [128, 16384], BF16)

    id_sb = nc.alloc_sbuf_tensor("id_sb", [128, 128], BF16)
    onesr = nc.alloc_sbuf_tensor("onesr", [1, 128], BF16)
    onesc = nc.alloc_sbuf_tensor("onesc", [128, 1], BF16)
    sel_sb = nc.alloc_sbuf_tensor("sel_sb", [32, D], BF16)
    sel8_sb = nc.alloc_sbuf_tensor("sel8_sb", [NE, NE * 128], BF16)
    bo_sb = nc.alloc_sbuf_tensor("bo_sb", [1, D], BF16)
    b2_sb = nc.alloc_sbuf_tensor("b2_sb", [NE, D], BF16)
    gwt_sb = nc.alloc_sbuf_tensor("gwt_sb", [128, NDT * NE], BF16)
    gbb_f = nc.alloc_sbuf_tensor("gbb_f", [1, 2 * NE], F32)
    gbb_sb = nc.alloc_sbuf_tensor("gbb_sb", [1, NE], BF16)
    b1row = nc.alloc_sbuf_tensor("b1row", [1, FF], BF16)
    b1col = nc.alloc_sbuf_tensor("b1col", [128, NHT], F32)
    kvacc = nc.alloc_sbuf_tensor("kvacc", [128, NPAIR * 128], F32)
    ksacc = nc.alloc_sbuf_tensor("ksacc", [128, NPAIR], F32)
    kvl = nc.alloc_sbuf_tensor("kvl", [128, NPAIR * 128], BF16)
    ks_sb = nc.alloc_sbuf_tensor("ks_sb", [128, NPAIR * 32], BF16)
    stats = nc.alloc_sbuf_tensor("stats", [128, 32], F32)
    rcp_f = nc.alloc_sbuf_tensor("rcp_f", [128, 32], F32)
    rcp_bf = nc.alloc_sbuf_tensor("rcp_bf", [128, NTT * 32], BF16)
    rcpT = nc.alloc_sbuf_tensor("rcpT", [32, TOK], BF16)
    crep_sb = nc.alloc_sbuf_tensor("crep_sb", [128, 1024], F32)
    comb_f = nc.alloc_sbuf_tensor("comb_f", [128, NTT * NE], F32)
    comb_bf = nc.alloc_sbuf_tensor("comb_bf", [128, NTT * NE], BF16)
    combT = nc.alloc_sbuf_tensor("combT", [NE, TOK], BF16)
    cscr = nc.alloc_sbuf_tensor("cscr", [128, 4 * NE], F32)
    knat = nc.alloc_sbuf_tensor("knat", [128, D], BF16)

    PS0 = nc.alloc_psum_tensor("PS0", [128, 2048], F32)
    PS1 = nc.alloc_psum_tensor("PS1", [128, 2048], F32)

    ctx = contextlib.ExitStack()
    sems = {}
    cnt = {}

    def getsem(name):
        if name not in sems:
            sems[name] = ctx.enter_context(nc.semaphore("sem_" + name))
            cnt[name] = 0
        return sems[name]

    for name in ("v", "a", "t", "o"):
        getsem(name)

    prog = {e: [] for e in ("sync", "gpsimd", "vector", "scalar", "tensor")}

    def emit(eng, fn):
        prog[eng].append(fn)

    def plan(sem, n=1):
        getsem(sem)
        cnt[sem] += n
        return cnt[sem]

    def wait(eng, sem, val):
        if val <= 0:
            return
        getsem(sem)
        assert val <= cnt[sem], (eng, sem, val, cnt[sem])
        if sem not in ("v", "a", "t"):
            assert val == cnt[sem] or val <= 0, (eng, sem, val, cnt[sem])
        if val <= 0:
            return
        h = sems[sem]
        emit(eng, lambda E, h=h, v=val: E.wait_ge(h, v))

    def run(eng, sem, fn, n=1):
        getsem(sem)
        v = plan(sem, n)
        h = sems[sem]
        emit(eng, lambda E, h=h, n=n, fn=fn: fn(E).then_inc(h, n))
        return v

    OWNER = {"t": "tensor", "v": "vector", "a": "scalar"}

    def gate_gpsimd(*conds):
        getsem("gg")
        for sem, val in conds:
            if val <= 0:
                continue
            assert sem in OWNER
            assert val <= cnt[sem]
            eng = OWNER[sem]
            hown = sems[sem]
            emit(eng, lambda E, hown=hown, v=val: E.wait_ge(hown, v))
            hgg = sems["gg"]
            plan("gg", 1)
            emit(eng, lambda E, hgg=hgg: E.nop().then_inc(hgg, 1))
        wait("gpsimd", "gg", cnt["gg"])

    wk_res = AR1[:].bitcast(BF16)

    def wv_chunk(dt):
        if dt < 8:
            return AR2[:, 2048 * dt:2048 * dt + 2048]
        return AR3[:, 2048 * (dt - 8):2048 * (dt - 8) + 2048]

    def x_tile(tt):
        return AR1[:, 2048 * tt:2048 * tt + 2048]

    def lnT_dt(dt):
        return AR2[:, 1024 * dt:1024 * dt + 1024]

    def qT_pair(p):
        return AR3[:, 1024 * p:1024 * p + 1024]

    def hs_ht(ht):
        return AR3[:, 1024 * ht:1024 * ht + 1024]

    ar4f = AR4[:].bitcast(F32)
    vnat = crep_sb[:].bitcast(BF16)

    def xk_slot(r):
        return ar4f[:, 2048 * r:2048 * r + 2048]

    def lnN_kv_slot(r):
        return AR4[:, 8192 + 2048 * r:8192 + 2048 * r + 2048]

    def lnTt_slot(r):
        return AR4[:, 12288 + 2048 * r:12288 + 2048 * r + 2048]

    def wq_half_dt(dt):
        if QDST == 1:
            return AR3[:, 1024 * (dt % 4) + 4096:1024 * (dt % 4) + 5120]
        return AR4[:, 1024 * dt:1024 * dt + 1024]

    def wo_slot(r):
        return AR4[:, 2048 * r:2048 * r + 2048]

    def w1_slot(r):
        return AR4[:, 1408 * r:1408 * r + 1408]

    def w2_slot(r):
        return AR4[:, 2816 + 2048 * r:2816 + 2048 * r + 2048]

    def silu_slot(ssl):
        return ar4f[:, 5504 + 1024 * ssl:5504 + 1024 * ssl + 1024]

    def lnN_own_slot(r):
        return knat[:] if r == 0 else vnat

    def bankv(ps, b, n=1):
        return ps[:, 512 * b:512 * b + 512 * n]

    run("sync", "sc", lambda E: E.dma_start(out=id_sb[:], in_=ident_in), 16)
    run("sync", "sc", lambda E: E.dma_start(out=onesr[:], in_=onesr_in), 16)
    run("sync", "sc", lambda E: E.dma_start(out=onesc[:], in_=onesc_in), 16)
    run("sync", "sc", lambda E: E.dma_start(out=sel_sb[:], in_=sel_in), 16)
    run("sync", "sc", lambda E: E.dma_start(out=sel8_sb[:], in_=sel8_in), 16)
    run("sync", "sc", lambda E: E.dma_start(out=gbb_f[:, 0:NE], in_=gate_b), 16)
    s_consts = run("sync", "sc",
                   lambda E: E.dma_start(out=gbb_f[:, NE:2 * NE], in_=bias_e), 16)

    run("gpsimd", "ss", lambda E: E.dma_start(out=bo_sb[:], in_=bo), 16)
    run("gpsimd", "ss", lambda E: E.dma_start(out=b2_sb[:], in_=b2), 16)
    g_small = run("gpsimd", "ss", lambda E: E.dma_start(
        out=gwt_sb[:].rearrange("p (dt e) -> p dt e", dt=NDT),
        in_=gwt.rearrange("(dt p) e -> p dt e", p=128)), 16)

    wait("vector", "sc", s_consts)
    run("vector", "v", lambda E: E.tensor_add(
        gbb_f[:, 0:NE], gbb_f[:, 0:NE], gbb_f[:, NE:2 * NE]))
    wait("vector", "v", cnt["v"])
    v_gbb = run("vector", "v", lambda E: E.tensor_copy(gbb_sb[:], gbb_f[:, 0:NE]))

    run("vector", "v", lambda E: E.memset(kvl[:], 0.0))
    v_zero = run("vector", "v", lambda E: E.memset(ks_sb[:], 0.0))

    for dt in range(NDT):
        run("gpsimd", "swkv", lambda E, dt=dt: E.dma_start(
            out=wk_res[:, 2048 * dt:2048 * dt + 2048],
            in_=wk[128 * dt:128 * dt + 128, :]), 16)
    for dt in range(NDT):
        run("gpsimd", "swkv", lambda E, dt=dt: E.dma_start(
            out=wv_chunk(dt), in_=wv[128 * dt:128 * dt + 128, :]), 16)
    g_wkv = cnt["swkv"]

    def flat(ap):
        r = ap
        while len(r.shape) > 1:
            r = r.rearrange("a b -> (a b)") if len(r.shape) == 2 else                 r.rearrange("a b c -> (a b) c")
        return r[None, :]

    run("gpsimd", "sbwq", lambda E: E.dma_start(out=flat(wqb), in_=flat(wq)), 16)
    run("gpsimd", "sbwo", lambda E: E.dma_start(out=flat(wob), in_=flat(wo)), 16)
    for e in range(NE):
        run("gpsimd", "sbw1", lambda E, e=e: E.dma_start(
            out=flat(w1b[e]), in_=flat(w1t[e])), 16)
        run("gpsimd", "sbw2", lambda E, e=e: E.dma_start(
            out=flat(w2b[e]), in_=flat(w2t[e])), 16)

    ln_prev = {"v": 0, "a": 0}

    def emit_ln_tile(src_ap, nat_slot, dst_writer):
        for c in range(4):
            run("vector", "v", lambda E, c=c: E.bn_stats(
                stats[:, 6 * c:6 * c + 6], src_ap[:, 512 * c:512 * c + 512]))
        wait("vector", "v", cnt["v"])
        run("vector", "v", lambda E: E.bn_aggr(stats[:, 24:26], stats[:, 0:24]))
        v_mv = cnt["v"]
        wait("vector", "v", v_mv)
        run("vector", "v", lambda E: E.tensor_scalar_add(
            stats[:, 30:31], stats[:, 25:26], EPS))
        v_ve = cnt["v"]
        wait("scalar", "v", v_ve)
        run("scalar", "a", lambda E: E.activation(
            stats[:, 26:27], stats[:, 30:31], AF.Sqrt))
        wait("vector", "a", cnt["a"])
        run("vector", "v", lambda E: E.reciprocal(stats[:, 27:28], stats[:, 26:27]))
        wait("vector", "v", cnt["v"])
        run("vector", "v", lambda E: E.tensor_mul(
            stats[:, 28:29], stats[:, 24:25], stats[:, 27:28]))
        wait("vector", "v", cnt["v"])
        v_ab = run("vector", "v", lambda E: E.tensor_scalar_mul(
            stats[:, 29:30], stats[:, 28:29], -1.0))
        wait("scalar", "v", v_ab)
        a_ln = run("scalar", "a", lambda E: E.activation(
            nat_slot, src_ap, AF.Identity,
            bias=stats[:, 29:30], scale=stats[:, 27:28]))
        wait("tensor", "a", a_ln)
        for j in range(4):
            ps = bankv(PS0, 3) if j % 2 == 0 else bankv(PS1, 3)
            eng = "vector" if j % 2 == 0 else "scalar"
            ek = "v" if j % 2 == 0 else "a"
            wait("tensor", ek, ln_prev[ek])
            for u in range(4):
                dtj = 4 * j + u
                run("tensor", "t", lambda E, ps=ps, u=u, dtj=dtj: E.matmul(
                    ps[:, 128 * u:128 * u + 128],
                    nat_slot[:, 128 * dtj:128 * dtj + 128],
                    id_sb[:], start=True, stop=True))
            t_pack = cnt["t"]
            wait(eng, "t", t_pack)
            dst_writer(j, ps, eng)
            ln_prev[ek] = cnt[ek]
        return a_ln

    xk_lnread_a = {}
    lnTt_done_t = {}
    kv_add_done = {}

    kv_range = range(KV_NTT if LIMIT >= 1 else 0)
    for kt in kv_range:
        r = kt % 2
        if kt >= 2:
            wait("sync", "a", xk_lnread_a[kt - 2])
        sxn = "sx%d" % r
        s_x = run("sync", sxn, lambda E, kt=kt, r=r: E.dma_start(
            out=xk_slot(r), in_=xkv[128 * kt:128 * kt + 128, :]), 16)
        wait("vector", sxn, s_x)
        wait("scalar", sxn, s_x)
        if kt >= 2:
            tprev = lnTt_done_t[kt - 2]
            wait("vector", "t", tprev)
            wait("scalar", "t", tprev)
        lt = kt % 2

        def lnTt_writer(j, ps, eng, lt=lt):
            dst = lnTt_slot(lt)[:, 512 * j:512 * j + 512]
            if eng == "vector":
                run("vector", "v", lambda E, ps=ps, dst=dst: E.tensor_copy(dst, ps))
            else:
                run("scalar", "a", lambda E, ps=ps, dst=dst: E.copy(dst, ps))

        a_ln = emit_ln_tile(xk_slot(r), lnN_kv_slot(lt), lnTt_writer)
        xk_lnread_a[kt] = a_ln
        v_lnTt = cnt["v"]
        a_lnTt = cnt["a"]

        if kt == 0:
            wait("tensor", "sc", s_consts)
            wait("tensor", "swkv", g_wkv)
        if kt >= 1:
            wait("tensor", "v", kv_add_done[kt - 1])
        wait("tensor", "v", v_lnTt)
        wait("tensor", "a", a_lnTt)
        for dt in range(NDT):
            lhs = lnTt_slot(lt)[:, 128 * dt:128 * dt + 128]
            for c in range(4):
                run("tensor", "t", lambda E, lhs=lhs, dt=dt, c=c: E.matmul(
                    bankv(PS0, c), lhs,
                    wk_res[:, 2048 * dt + 512 * c:2048 * dt + 512 * c + 512],
                    start=(dt == 0), stop=(dt == NDT - 1)))
            for c in range(4):
                run("tensor", "t", lambda E, lhs=lhs, dt=dt, c=c: E.matmul(
                    bankv(PS1, c), lhs,
                    wv_chunk(dt)[:, 512 * c:512 * c + 512],
                    start=(dt == 0), stop=(dt == NDT - 1)))
        t_proj = cnt["t"]
        lnTt_done_t[kt] = t_proj

        wait("scalar", "t", t_proj)
        a_k = run("scalar", "a", lambda E: E.activation(knat[:], PS0[:], AF.Relu))
        wait("vector", "t", t_proj)
        v_v = run("vector", "v", lambda E: E.tensor_copy(vnat, PS1[:]))

        wait("tensor", "a", a_k)
        wait("tensor", "v", v_v)
        for p in range(NPAIR):
            ps = PS0 if p < 8 else PS1
            q = p % 8
            run("tensor", "t", lambda E, ps=ps, p=p, q=q: E.matmul(
                ps[:, 128 * q:128 * q + 128],
                knat[:, 128 * p:128 * p + 128],
                vnat[:, 128 * p:128 * p + 128], start=True, stop=True))
            run("tensor", "t", lambda E, ps=ps, p=p, q=q: E.matmul(
                ps[:, 1024 + q:1024 + q + 1],
                knat[:, 128 * p:128 * p + 128],
                onesc[:], start=True, stop=True))
        t_kv = cnt["t"]

        wait("vector", "t", t_kv)
        if kt == 0:
            run("vector", "v", lambda E: E.tensor_copy(kvacc[:, 0:1024],
                                                       PS0[:, 0:1024]))
            run("vector", "v", lambda E: E.tensor_copy(kvacc[:, 1024:2048],
                                                       PS1[:, 0:1024]))
            run("vector", "v", lambda E: E.tensor_copy(ksacc[:, 0:8],
                                                       PS0[:, 1024:1032]))
            run("vector", "v", lambda E: E.tensor_copy(ksacc[:, 8:16],
                                                       PS1[:, 1024:1032]))
        else:
            run("vector", "v", lambda E: E.tensor_add(
                kvacc[:, 0:1024], PS0[:, 0:1024], kvacc[:, 0:1024]))
            run("vector", "v", lambda E: E.tensor_add(
                kvacc[:, 1024:2048], PS1[:, 0:1024], kvacc[:, 1024:2048]))
            run("vector", "v", lambda E: E.tensor_add(
                ksacc[:, 0:8], PS0[:, 1024:1032], ksacc[:, 0:8]))
            run("vector", "v", lambda E: E.tensor_add(
                ksacc[:, 8:16], PS1[:, 1024:1032], ksacc[:, 8:16]))
        kv_add_done[kt] = cnt["v"]

    t_kv_final = cnt["t"]
    v_kv_final = cnt["v"]

    wait("vector", "v", v_kv_final)
    wait("vector", "v", v_zero)
    for p in range(NPAIR if LIMIT >= 1 else 0):
        c0 = 128 * p
        run("vector", "v", lambda E, c0=c0: E.tensor_copy(
            kvl[0:64, c0:c0 + 64], kvacc[0:64, c0:c0 + 64]))
        run("vector", "v", lambda E, c0=c0: E.tensor_copy(
            kvl[64:128, c0 + 64:c0 + 128], kvacc[64:128, c0 + 64:c0 + 128]))
        k0 = 32 * p
        run("vector", "v", lambda E, p=p, k0=k0: E.tensor_copy(
            ks_sb[0:64, k0 + 2 * p:k0 + 2 * p + 1], ksacc[0:64, p:p + 1]))
        run("vector", "v", lambda E, p=p, k0=k0: E.tensor_copy(
            ks_sb[64:128, k0 + 2 * p + 1:k0 + 2 * p + 2],
            ksacc[64:128, p:p + 1]))
    v_kvl = cnt["v"]

    wait("sync", "t", t_kv_final)
    s_xs = run("sync", "sxs", lambda E: E.dma_start(
        out=AR1[:].rearrange("p (tt d) -> p tt d", tt=NTT),
        in_=xs.rearrange("(tt p) d -> p tt d", p=128)), 16)

    wait("vector", "t", t_kv_final)
    wait("scalar", "t", t_kv_final)
    wait("vector", "sxs", s_xs)
    wait("scalar", "sxs", s_xs)
    x_res_done = {tt: 0 for tt in range(NTT)}
    ln_marks = {}

    def emit_own_ln(tt, prev_mark):
        lt = tt % 2

        def wtr(j, ps, eng, tt=tt):
            for u in range(4):
                dtj = 4 * j + u
                dst = lnT_dt(dtj)[:, 128 * tt:128 * tt + 128]
                src = ps[:, 128 * u:128 * u + 128]
                if eng == "vector":
                    run("vector", "v", lambda E, s=src, d=dst: E.tensor_copy(d, s))
                else:
                    run("scalar", "a", lambda E, s=src, d=dst: E.copy(d, s))

        if prev_mark is not None:
            wait("vector", "t", prev_mark)
            wait("scalar", "t", prev_mark)
        emit_ln_tile(x_tile(tt), lnN_own_slot(lt), wtr)
        return cnt["t"]

    for tt in range(NTT if LIMIT >= 2 else 0):
        ln_marks[tt] = emit_own_ln(tt, ln_marks.get(tt - 2))
    v_ln1 = cnt["v"]
    a_ln1 = cnt["a"]
    if LIMIT < 2:
        emit_attn = False
    else:
        emit_attn = True

    class _StopAttn(Exception):
        pass

    v_x_attn = 0
    try:
        if not (emit_attn and SUB >= 1):
                raise _StopAttn()
        wait("sync", "a", xk_lnread_a.get(KV_NTT - 1, 0))
        wait("sync", "t", t_kv_final)
        q_evac_a = {}
        for h in range(QH):
            if h == 0:
                wait("sync", "sbwq", 16)
            if h == 1:
                wait("sync", "t", cnt["t"])
            if QLOAD:
                for dt in range(NDT):
                    run("sync", "swq", lambda E, h=h, dt=dt: E.dma_start(
                        out=wq_half_dt(dt),
                        in_=wqb[128 * dt:128 * dt + 128,
                                1024 * h:1024 * h + 1024]), 16)
            g_wq = cnt.get("swq", 0)
            if g_wq:
                wait("tensor", "swq", g_wq)
            if h == 0:
                wait("tensor", "v", v_ln1)
                wait("tensor", "a", a_ln1)
            for m in range(QMT):
                pg = 8 * h + m
                ps = PS0 if pg % 2 == 0 else PS1
                if pg >= 2:
                    wait("tensor", "a", q_evac_a[pg - 2])
                for dt in range(NDT):
                    lhs = wq_half_dt(dt)[:, 128 * m:128 * m + 128]
                    for c in range(2):
                        run("tensor", "t", lambda E, ps=ps, lhs=lhs, dt=dt, c=c:
                            E.matmul(bankv(ps, c), lhs,
                                     lnT_dt(dt)[:, 512 * c:512 * c + 512],
                                     start=(dt == 0), stop=(dt == NDT - 1)))
                t_q = cnt["t"]
                wait("scalar", "t", t_q)
                q_evac_a[pg] = run("scalar", "a", lambda E, ps=ps, pg=pg:
                                   E.activation(qT_pair(pg), ps[:, 0:1024], AF.Relu))
        a_q = cnt["a"]

        if SUB < 2:
            raise _StopAttn()
        wait("tensor", "a", a_q)
        wait("tensor", "v", v_kvl)
        for tt in range(NTT):
            for p in range(NPAIR):
                run("tensor", "t", lambda E, tt=tt, p=p: E.matmul(
                    PS1[:, 1536 + 32 * tt:1536 + 32 * tt + 32],
                    qT_pair(p)[:, 128 * tt:128 * tt + 128],
                    ks_sb[:, 32 * p:32 * p + 32],
                    start=(p == 0), stop=(p == NPAIR - 1)))
        t_norm = cnt["t"]

        wait("vector", "t", t_norm)
        for tt in range(NTT):
            wait("vector", "v", cnt["v"])
            run("vector", "v", lambda E, tt=tt: E.tensor_scalar_max(
                rcp_f[:], PS1[:, 1536 + 32 * tt:1536 + 32 * tt + 32], 1e-6))
            wait("vector", "v", cnt["v"])
            run("vector", "v", lambda E: E.reciprocal(rcp_f[:], rcp_f[:]))
            wait("vector", "v", cnt["v"])
            run("vector", "v", lambda E, tt=tt: E.tensor_copy(
                rcp_bf[:, 32 * tt:32 * tt + 32], rcp_f[:]))
        v_rcp = cnt["v"]

        wait("tensor", "v", v_rcp)
        for tt in range(NTT):
            run("tensor", "t", lambda E, tt=tt: E.matmul(
                PS1[0:32, 1536:1664], rcp_bf[:, 32 * tt:32 * tt + 32],
                id_sb[:], start=True, stop=True))
            t_r = cnt["t"]
            wait("vector", "t", t_r)
            v_r = run("vector", "v", lambda E, tt=tt: E.tensor_copy(
                rcpT[0:32, 128 * tt:128 * tt + 128], PS1[0:32, 1536:1664]))
            wait("tensor", "v", v_r)
        v_rcpT = cnt["v"]

        if SUB < 3:
            raise _StopAttn()
        attnT_v = {}
        rep_a = {}
        for p in range(NPAIR):
            ps_q = PS0 if p % 2 == 0 else PS1
            if p >= 2:
                wait("tensor", "v", attnT_v[p - 2])
            for c in range(2):
                run("tensor", "t", lambda E, ps_q=ps_q, p=p, c=c: E.matmul(
                    bankv(ps_q, c), kvl[:, 128 * p:128 * p + 128],
                    qT_pair(p)[:, 512 * c:512 * c + 512], start=True, stop=True))
            for c in range(2):
                run("tensor", "t", lambda E, ps_q=ps_q, p=p, c=c: E.matmul(
                    ps_q[:, 1024 + 512 * c:1024 + 512 * c + 512],
                    sel_sb[:, 128 * p:128 * p + 128],
                    rcpT[0:32, 512 * c:512 * c + 512], start=True, stop=True))
            t_p = cnt["t"]
            wait("scalar", "t", t_p)
            if p >= 1:
                wait("scalar", "v", attnT_v[p - 1])
            rep_a[p] = run("scalar", "a", lambda E, ps_q=ps_q: E.copy(
                ar4f[:, 0:1024], ps_q[:, 1024:2048]))
            wait("vector", "t", t_p)
            wait("vector", "a", rep_a[p])
            attnT_v[p] = run("vector", "v", lambda E, ps_q=ps_q, p=p: E.tensor_mul(
                qT_pair(p), ps_q[:, 0:1024], ar4f[:, 0:1024]))
        v_attnT = cnt["v"]

        if SUB < 4:
            raise _StopAttn()
        wait("sync", "t", cnt["t"])
        wait("sync", "v", v_attnT)
        wait("sync", "sbwo", 16)
        x_res_done = {}
        wo_cons = {}
        wait("tensor", "v", v_attnT)
        for g in range(4):
            if g >= 1:
                wait("tensor", "v", x_res_done[2 * g - 1])
            wait("tensor", "ss", g_small)
            for u in range(2):
                ps = PS0 if u == 0 else PS1
                for c in range(4):
                    run("tensor", "t", lambda E, ps=ps, c=c: E.matmul(
                        bankv(ps, c), onesr[0:1, :],
                        bo_sb[0:1, 512 * c:512 * c + 512], start=True, stop=False))
            for p in range(NPAIR):
                idx = g * NPAIR + p
                pm = wo_cons.get(idx - 4)
                if pm is not None:
                    wait("sync", "t", pm)
                swon = "swo%d" % (p % 4)
                gm = run("sync", swon, lambda E, p=p: E.dma_start(
                    out=wo_slot(p % 4), in_=wob[128 * p:128 * p + 128, :]), 16)
                wait("tensor", swon, gm)
                for u in range(2):
                    tt = 2 * g + u
                    ps = PS0 if u == 0 else PS1
                    lhs = qT_pair(p)[:, 128 * tt:128 * tt + 128]
                    last = (p == NPAIR - 1)
                    for c in range(4):
                        run("tensor", "t",
                            lambda E, ps=ps, lhs=lhs, p=p, c=c, last=last:
                            E.matmul(bankv(ps, c), lhs,
                                     wo_slot(p % 4)[:, 512 * c:512 * c + 512],
                                     start=False, stop=last))
                wo_cons[idx] = cnt["t"]
            t_g = cnt["t"]
            for u in range(2):
                tt = 2 * g + u
                ps = PS0 if u == 0 else PS1
                wait("vector", "t", t_g)
                wait("vector", "sxs", s_xs)
                x_res_done[tt] = run("vector", "v", lambda E, ps=ps, tt=tt:
                                     E.tensor_add(x_tile(tt), ps[:], x_tile(tt)))
        v_x_attn = cnt["v"]
    except _StopAttn:
        pass

    wait("tensor", "v", v_x_attn)
    ln2_marks = {}
    for tt in range(NTT if LIMIT >= 3 else 0):
        lt = tt % 2

        def wtr2(j, ps, eng, tt=tt):
            for u in range(4):
                dtj = 4 * j + u
                dst = lnT_dt(dtj)[:, 128 * tt:128 * tt + 128]
                src = ps[:, 128 * u:128 * u + 128]
                if eng == "vector":
                    run("vector", "v", lambda E, s=src, d=dst: E.tensor_copy(d, s))
                else:
                    run("scalar", "a", lambda E, s=src, d=dst: E.copy(d, s))

        wait("vector", "v", x_res_done[tt])
        wait("scalar", "v", x_res_done[tt])
        pm = ln2_marks.get(tt - 2)
        if pm is not None:
            wait("vector", "t", pm)
            wait("scalar", "t", pm)
        emit_ln_tile(x_tile(tt), lnN_own_slot(lt), wtr2)
        ln2_marks[tt] = cnt["t"]
    v_ln2 = cnt["v"]
    a_ln2 = cnt["a"]

    wait("tensor", "v", v_ln2)
    wait("tensor", "a", a_ln2)
    wait("tensor", "v", v_gbb)
    comb_read_v = {}
    for tt in range(NTT if LIMIT >= 3 else 0):
        if tt >= 1:
            wait("tensor", "v", comb_read_v[tt - 1])
        for dt in range(NDT):
            run("tensor", "t", lambda E, tt=tt, dt=dt: E.matmul(
                PS0[:, 8 * tt:8 * tt + 8],
                lnT_dt(dt)[:, 128 * tt:128 * tt + 128],
                gwt_sb[:, 8 * dt:8 * dt + 8],
                start=(dt == 0), stop=False))
        run("tensor", "t", lambda E, tt=tt: E.matmul(
            PS0[:, 8 * tt:8 * tt + 8], onesr[0:1, :],
            gbb_sb[0:1, :], start=False, stop=True))
        t_sc = cnt["t"]
        sc = PS0[:, 8 * tt:8 * tt + 8]
        m1 = cscr[:, 0:1]
        m2 = cscr[:, 1:2]
        g1 = cscr[:, 2:3]
        g2 = cscr[:, 3:4]
        is1 = cscr[:, 8:16]
        msk = cscr[:, 16:24]
        t2 = cscr[:, 24:32]
        wait("vector", "t", t_sc)
        run("vector", "v", lambda E, sc=sc, m1=m1: E.reduce_max(m1, sc, axis=AX.X))
        wait("vector", "v", cnt["v"])
        run("vector", "v", lambda E, sc=sc, is1=is1, m1=m1: E.tensor_scalar(
            is1, sc, m1, None, ALU.is_ge))
        wait("vector", "v", cnt["v"])
        comb_read_v[tt] = run("vector", "v",
                              lambda E, sc=sc, is1=is1, msk=msk:
                              E.scalar_tensor_tensor(msk, is1, -BIG, sc,
                                                     ALU.mult, ALU.add))
        wait("vector", "v", cnt["v"])
        run("vector", "v", lambda E, msk=msk, m2=m2: E.reduce_max(
            m2, msk, axis=AX.X))
        wait("vector", "v", cnt["v"])
        v_d = run("vector", "v", lambda E, m1=m1, m2=m2, g1=g1:
                  E.tensor_sub(g1, m1, m2))
        wait("scalar", "v", v_d)
        a_g1 = run("scalar", "a", lambda E, g1=g1: E.activation(
            g1, g1, AF.Sigmoid))
        wait("vector", "a", a_g1)
        run("vector", "v", lambda E, g1=g1, g2=g2: E.tensor_scalar(
            g2, g1, -1.0, 1.0, ALU.mult, ALU.add))
        wait("vector", "v", cnt["v"])
        run("vector", "v", lambda E, is1=is1, g1=g1: E.tensor_scalar(
            is1, is1, g1, None, ALU.mult))
        run("vector", "v", lambda E, msk=msk, m2=m2, t2=t2: E.tensor_scalar(
            t2, msk, m2, None, ALU.is_ge))
        wait("vector", "v", cnt["v"])
        run("vector", "v", lambda E, t2=t2, g2=g2: E.tensor_scalar(
            t2, t2, g2, None, ALU.mult))
        wait("vector", "v", cnt["v"])
        run("vector", "v", lambda E, tt=tt, is1=is1, t2=t2: E.tensor_add(
            comb_f[:, 8 * tt:8 * tt + 8], is1, t2))
        wait("vector", "v", cnt["v"])
        run("vector", "v", lambda E, tt=tt: E.tensor_copy(
            comb_bf[:, 8 * tt:8 * tt + 8], comb_f[:, 8 * tt:8 * tt + 8]))
    v_comb = cnt["v"]

    wait("tensor", "v", v_comb)
    for tt in range(NTT if LIMIT >= 3 else 0):
        run("tensor", "t", lambda E, tt=tt: E.matmul(
            PS1[0:8, 0:128], comb_bf[:, 8 * tt:8 * tt + 8], id_sb[:],
            start=True, stop=True))
        t_c = cnt["t"]
        wait("vector", "t", t_c)
        v_c = run("vector", "v", lambda E, tt=tt: E.tensor_copy(
            combT[0:8, 128 * tt:128 * tt + 128], PS1[0:8, 0:128]))
        wait("tensor", "v", v_c)
    v_combT = cnt["v"]

    wait("sync", "t", cnt["t"])
    wait("sync", "sbw1", 16 * NE)
    wait("sync", "sbw2", 16 * NE)
    wait("vector", "a", a_ln2)

    w1_ring_mark = {}
    w2_ring_mark = {}
    x_moe_done = dict(x_res_done)
    prev_hs_v = None
    b1_read_t = None
    silu_mark = {}
    for e in range(MOEE if LIMIT >= 4 else 0):
        wait("tensor", "v", cnt["v"])
        if b1_read_t is not None:
            gate_gpsimd(("t", b1_read_t))
        g_b1 = run("gpsimd", "sb1", lambda E, e=e: E.dma_start(
            out=b1row[:], in_=b1[e:e + 1, :]), 16)

        run("tensor", "t", lambda E, e=e: E.matmul(
            PS0[:, 0:512], sel8_sb[:, 128 * e:128 * e + 128],
            combT[0:8, 0:512], start=True, stop=True))
        run("tensor", "t", lambda E, e=e: E.matmul(
            PS0[:, 512:1024], sel8_sb[:, 128 * e:128 * e + 128],
            combT[0:8, 512:1024], start=True, stop=True))
        t_rep = cnt["t"]
        wait("scalar", "t", t_rep)
        if prev_hs_v is not None:
            wait("scalar", "v", prev_hs_v)
        a_crep = run("scalar", "a", lambda E: E.copy(crep_sb[:], PS0[:, 0:1024]))

        wait("tensor", "sb1", g_b1)
        for ht in range(NHT):
            run("tensor", "t", lambda E, ht=ht: E.matmul(
                PS1[:, ht:ht + 1], b1row[0:1, 128 * ht:128 * ht + 128],
                onesc[0:1, 0:1], start=True, stop=True))
        b1_read_t = cnt["t"]
        wait("vector", "t", b1_read_t)
        v_b1 = run("vector", "v", lambda E: E.tensor_copy(b1col[:], PS1[:, 0:NHT]))

        wait("tensor", "a", a_crep)
        wait("tensor", "v", v_b1)
        hs_v = {}
        for gi, (h0, h1) in enumerate(HTG):
            nsl = h1 - h0
            slots = [(PS0, 0), (PS0, 2), (PS1, 0), (PS1, 2)][:nsl]
            for dt in range(NDT):
                i = (e * len(HTG) + gi) * NDT + dt
                slot = i % 2
                pm = w1_ring_mark.get(i - 2)
                if pm is not None:
                    wait("sync", "t", pm)
                sw1n = "sw1_%d" % slot
                gm = run("sync", sw1n, lambda E, e=e, dt=dt, slot=slot:
                         E.dma_start(out=w1_slot(slot),
                                     in_=w1b[e, 128 * dt:128 * dt + 128, :]), 16)
                wait("tensor", sw1n, gm)
                for si, ht in enumerate(range(h0, h1)):
                    ps, b0 = slots[si]
                    lhs = w1_slot(slot)[:, 128 * ht:128 * ht + 128]
                    for c in range(2):
                        run("tensor", "t",
                            lambda E, ps=ps, b0=b0, lhs=lhs, dt=dt, c=c:
                            E.matmul(
                                ps[:, 512 * (b0 + c):512 * (b0 + c) + 512],
                                lhs, lnT_dt(dt)[:, 512 * c:512 * c + 512],
                                start=(dt == 0), stop=(dt == NDT - 1)))
                w1_ring_mark[i] = cnt["t"]
            t_w1g = cnt["t"]
            for si, ht in enumerate(range(h0, h1)):
                ps, b0 = slots[si]
                ssl = si % 2
                st = silu_slot(ssl)
                wait("scalar", "t", t_w1g)
                wait("scalar", "v", v_b1)
                sm = silu_mark.get(ssl)
                if sm is not None:
                    wait("scalar", "v", sm)
                a_s = run("scalar", "a", lambda E, ps=ps, b0=b0, st=st, ht=ht:
                          E.activation(st, ps[:, 512 * b0:512 * b0 + 1024],
                                       AF.Silu, bias=b1col[:, ht:ht + 1]))
                wait("vector", "a", a_s)
                hs_v[ht] = run("vector", "v", lambda E, st=st, ht=ht:
                               E.tensor_mul(hs_ht(ht), st, crep_sb[:]))
                silu_mark[ssl] = hs_v[ht]
            wait("tensor", "a", cnt["a"])
        prev_hs_v = cnt["v"]

        wait("tensor", "v", prev_hs_v)
        for ttg in range(4):
            if e > 0 or ttg > 0:
                wait("tensor", "v", cnt["v"])
            if e == 0:
                for u in range(2):
                    tt = 2 * ttg + u
                    ps = PS0 if u == 0 else PS1
                    for c in range(4):
                        run("tensor", "t", lambda E, ps=ps, tt=tt, c=c:
                            E.matmul(bankv(ps, c),
                                     combT[0:8, 128 * tt:128 * tt + 128],
                                     b2_sb[0:8, 512 * c:512 * c + 512],
                                     start=True, stop=False))
            for ht in range(NHT):
                i = (e * 4 + ttg) * NHT + ht
                slot = i % 4
                pm = w2_ring_mark.get(i - 4)
                if pm is not None:
                    wait("sync", "t", pm)
                sw2n = "sw2_%d" % slot
                gm = run("sync", sw2n, lambda E, e=e, ht=ht, slot=slot:
                         E.dma_start(out=w2_slot(slot),
                                     in_=w2b[e, 128 * ht:128 * ht + 128, :]), 16)
                wait("tensor", sw2n, gm)
                for u in range(2):
                    tt = 2 * ttg + u
                    ps = PS0 if u == 0 else PS1
                    lhs = hs_ht(ht)[:, 128 * tt:128 * tt + 128]
                    first = (e > 0 and ht == 0)
                    last = (ht == NHT - 1)
                    for c in range(4):
                        run("tensor", "t",
                            lambda E, ps=ps, lhs=lhs, slot=slot, c=c,
                            first=first, last=last:
                            E.matmul(bankv(ps, c), lhs,
                                     w2_slot(slot)[:, 512 * c:512 * c + 512],
                                     start=first, stop=last))
                w2_ring_mark[i] = cnt["t"]
            t_w2g = cnt["t"]
            for u in range(2):
                tt = 2 * ttg + u
                ps = PS0 if u == 0 else PS1
                wait("vector", "t", t_w2g)
                wait("vector", "v", x_moe_done[tt])
                x_moe_done[tt] = run("vector", "v", lambda E, ps=ps, tt=tt:
                                     E.tensor_add(x_tile(tt), ps[:],
                                                  x_tile(tt)))

    if DBG is not None:
        dbg_map = {"AR2": (AR2, BF16, 16384), "AR3": (AR3, BF16, 16384),
                   "AR1": (AR1, F32, 16384),
                   "kvacc": (kvacc, F32, NPAIR * 128),
                   "ksacc": (ksacc, F32, NPAIR),
                   "rcpT": (rcpT, BF16, TOK)}
        dt_, ddt, dcols = dbg_map[DBG]
        dbg_out = nc.dram_tensor("dbg", list(dt_.shape), ddt,
                                 kind="ExternalOutput").ap()
        wait("sync", "v", cnt["v"])
        wait("sync", "a", cnt["a"])
        wait("sync", "t", cnt["t"])
        run("sync", "o", lambda E: E.dma_start(out=dbg_out, in_=dt_[:]), 16)
    for tt in range(NTT):
        wait("sync", "v", x_moe_done[tt])
        run("sync", "o", lambda E, tt=tt: E.dma_start(
            out=out[128 * tt:128 * tt + 128, :], in_=x_tile(tt)), 16)
    wait("sync", "o", cnt["o"])

    with nc.Block() as block:
        @block.sync
        def _(E):
            for fn in prog["sync"]:
                fn(E)

        @block.gpsimd
        def _(E):
            for fn in prog["gpsimd"]:
                fn(E)

        @block.vector
        def _(E):
            for fn in prog["vector"]:
                fn(E)

        @block.scalar
        def _(E):
            for fn in prog["scalar"]:
                fn(E)

        @block.tensor
        def _(E):
            for fn in prog["tensor"]:
                fn(E)

    return nc


_NC_CACHE = []


DBG_OUT = None


def kernel(L0_attn_hidden_states, L0_attn_Wq, L0_attn_Wk, L0_attn_Wv,
           L0_attn_Wo, L0_attn_bo, L0_ffn_gate_w, L0_ffn_gate_b, L0_ffn_bias,
           L0_ffn_w1, L0_ffn_b1, L0_ffn_w2, L0_ffn_b2):
    x = np.ascontiguousarray(np.asarray(L0_attn_hidden_states, dtype=np.float32)
                             .reshape(KV_TOK, D))
    wq_h = np.ascontiguousarray(np.asarray(L0_attn_Wq, dtype=np.float32))
    wk_h = np.ascontiguousarray(np.asarray(L0_attn_Wk, dtype=np.float32))
    wv_h = np.ascontiguousarray(np.asarray(L0_attn_Wv, dtype=np.float32))
    wo_h = np.ascontiguousarray(np.asarray(L0_attn_Wo, dtype=np.float32))
    bo_h = np.ascontiguousarray(np.asarray(L0_attn_bo, dtype=np.float32)
                                .reshape(1, D))
    gate_w = np.asarray(L0_ffn_gate_w, dtype=np.float32)
    gwt_h = np.ascontiguousarray(gate_w.T)
    gate_b_h = np.ascontiguousarray(np.asarray(L0_ffn_gate_b, np.float32)
                                    .reshape(1, NE))
    bias_h = np.ascontiguousarray(np.asarray(L0_ffn_bias, np.float32)
                                  .reshape(1, NE))
    w1_h = np.asarray(L0_ffn_w1, dtype=np.float32)
    w1t_h = np.ascontiguousarray(w1_h.transpose(0, 2, 1))
    b1_h = np.ascontiguousarray(np.asarray(L0_ffn_b1, np.float32))
    w2_h = np.asarray(L0_ffn_w2, dtype=np.float32)
    w2t_h = np.ascontiguousarray(w2_h.transpose(0, 2, 1))
    b2_h = np.ascontiguousarray(np.asarray(L0_ffn_b2, np.float32))

    ident = np.eye(128, dtype=ml_dtypes.bfloat16)
    ones_row = np.ones((1, 128), dtype=ml_dtypes.bfloat16)
    ones_col = np.ones((128, 1), dtype=ml_dtypes.bfloat16)
    sel = np.zeros((32, D), dtype=ml_dtypes.bfloat16)
    for p in range(NPAIR):
        sel[2 * p, 128 * p:128 * p + 64] = 1
        sel[2 * p + 1, 128 * p + 64:128 * p + 128] = 1
    sel8 = np.zeros((NE, NE * 128), dtype=ml_dtypes.bfloat16)
    for e in range(NE):
        sel8[e, 128 * e:128 * e + 128] = 1

    if not _NC_CACHE:
        _NC_CACHE.append(build_nc())
    nc = _NC_CACHE[0]

    common = dict(xkv=x, wq=wq_h, wk=wk_h, wv=wv_h, wo=wo_h, bo=bo_h,
                  gwt=gwt_h, gate_b=gate_b_h, bias_e=bias_h, w1t=w1t_h,
                  b1=b1_h, w2t=w2t_h, b2=b2_h, ident=ident,
                  ones_row=ones_row, ones_col=ones_col, sel=sel, sel8=sel8)
    in_maps = []
    for c in range(NCORES):
        m = dict(common)
        m["xs"] = np.ascontiguousarray(x[TOK * c:TOK * c + TOK, :])
        in_maps.append(m)

    res = run_bass_kernel_spmd(nc, in_maps, core_ids=list(range(NCORES)))
    if DBG is not None:
        global DBG_OUT
        DBG_OUT = [np.asarray(res.results[c]["dbg"]) for c in range(NCORES)]
    full = np.concatenate([np.asarray(res.results[c]["out"])
                           for c in range(NCORES)], axis=0)
    return full.reshape(1, KV_TOK, D)
